# version 01
# baseline (speedup 1.0000x reference)
import contextlib
import numpy as np
import ml_dtypes
import concourse.bass as bass
import concourse.mybir as mybir
from concourse.bass_utils import run_bass_kernel_spmd

F32 = mybir.dt.float32
BF16 = mybir.dt.bfloat16
AF = mybir.ActivationFunctionType
AL = mybir.AluOpType
AX = mybir.AxisListType.X
BF = ml_dtypes.bfloat16

NCORE = 8
TOK = 1024
NT = 8
D = 4096
KD = 32


class Em:
    def __init__(self):
        self.ops = []

    def op(self, eng, fn, inc=1):
        self.ops.append((eng, fn, inc))

    def v(self, fn):
        self.op('vector', fn, 1)

    def s(self, fn):
        self.op('scalar', fn, 1)

    def t(self, fn):
        self.op('tensor', fn, 1)

    def dma(self, out, in_, **kw):
        self.op('sync', lambda e, o=out, i=in_, k=kw: e.dma_start(o, i, **k), 16)

    def build(self, nc, sem):
        engines = ['sync', 'vector', 'scalar', 'tensor', 'gpsimd']
        with nc.Block() as block:
            def mk(name):
                def f(eng):
                    cum = 0
                    for (ename, fn, inc) in self.ops:
                        if ename == name:
                            if cum > 0:
                                eng.wait_ge(sem, cum)
                            inst = fn(eng)
                            inst.then_inc(sem, inc)
                        cum += inc
                    eng.wait_ge(sem, cum)
                return f
            for name in engines:
                if any(e == name for (e, _, _) in self.ops):
                    getattr(block, name)(mk(name))


class B:
    def __init__(self):
        self.nc = bass.Bass("TRN2", target_bir_lowering=False)
        self.em = Em()
        self.stack = contextlib.ExitStack()
        self._n = 0

    def inp(self, name, shape, dt=F32):
        return self.nc.dram_tensor(name, list(shape), dt, kind="ExternalInput").ap()

    def outp(self, name, shape, dt=F32):
        return self.nc.dram_tensor(name, list(shape), dt, kind="ExternalOutput").ap()

    def scr(self, name, shape, dt=F32):
        return self.nc.dram_tensor(name, list(shape), dt, kind="Internal").ap()

    def sb(self, shape, dt=F32):
        self._n += 1
        return self.stack.enter_context(self.nc.sbuf_tensor(f"sb{self._n}", list(shape), dt))

    def ps(self, shape, dt=F32):
        self._n += 1
        return self.stack.enter_context(self.nc.psum_tensor(f"ps{self._n}", list(shape), dt))

    @staticmethod
    def _a(x):
        return x[:] if hasattr(x, 'alloc') or type(x).__name__ == 'SBTensorHandle' else x

    def tt(self, out, a, b_, op):
        out, a, b_ = B._a(out), B._a(a), B._a(b_)
        self.em.v(lambda e, o=out, x=a, y=b_, p=op: e.tensor_tensor(o, x, y, p))

    def ts(self, out, a, s1, s2, op0, op1=None):
        out, a = B._a(out), B._a(a)
        s1 = B._a(s1) if not isinstance(s1, (int, float)) else s1
        if s2 is not None and not isinstance(s2, (int, float)):
            s2 = B._a(s2)
        if op1 is None:
            self.em.v(lambda e, o=out, x=a, u=s1, p=op0: e.tensor_scalar(o, x, u, None, p))
        else:
            self.em.v(lambda e, o=out, x=a, u=s1, w=s2, p=op0, q=op1:
                      e.tensor_scalar(o, x, u, w, p, q))

    def stt(self, out, in0, scalar, in1, op0, op1):
        out, in0, in1 = B._a(out), B._a(in0), B._a(in1)
        if not isinstance(scalar, (int, float)):
            scalar = B._a(scalar)
        self.em.v(lambda e, o=out, x=in0, s=scalar, y=in1, p=op0, q=op1:
                  e.scalar_tensor_tensor(o, x, s, y, p, q))

    def red(self, out, a, op):
        out, a = B._a(out), B._a(a)
        self.em.v(lambda e, o=out, x=a, p=op: e.tensor_reduce(o, x, AX, p))

    def act(self, out, a, func, bias=None, scale=None, accum=None):
        out, a = B._a(out), B._a(a)
        if bias is not None and not isinstance(bias, (int, float)):
            bias = B._a(bias)
        if accum is not None:
            accum = B._a(accum)
        kw = {}
        if bias is not None:
            kw['bias'] = bias
        if scale is not None:
            kw['scale'] = scale
        if accum is not None:
            kw['accum_out'] = accum
        self.em.s(lambda e, o=out, x=a, f=func, k=kw: e.activation(o, x, f, **k))

    def vcopy(self, out, a):
        out, a = B._a(out), B._a(a)
        self.em.v(lambda e, o=out, x=a: e.tensor_copy(o, x))

    def scopy(self, out, a):
        out, a = B._a(out), B._a(a)
        self.em.s(lambda e, o=out, x=a: e.activation(o, x, AF.Copy))

    def memset(self, ap, c):
        ap = B._a(ap)
        self.em.v(lambda e, o=ap, v=c: e.memset(o, v))

    def recip(self, out, a):
        out, a = B._a(out), B._a(a)
        self.em.v(lambda e, o=out, x=a: e.reciprocal(o, x))

    def scan(self, out, d0, d1, init):
        out, d0, d1, init = B._a(out), B._a(d0), B._a(d1), B._a(init)
        self.em.v(lambda e, o=out, a=d0, b_=d1, i=init:
                  e.tensor_tensor_scan(o, a, b_, i, AL.mult, AL.add))

    def mm(self, out, lhsT, rhs, start, stop):
        out, lhsT, rhs = B._a(out), B._a(lhsT), B._a(rhs)
        self.em.t(lambda e, o=out, l=lhsT, r=rhs, s1=start, s2=stop:
                  e.matmul(o, l, r, start=s1, stop=s2))

    def tr(self, out, a, ident):
        out, a, ident = B._a(out), B._a(a), B._a(ident)
        self.em.t(lambda e, o=out, x=a, i=ident: e.transpose(o, x, i))

    def dma(self, out, in_, **kw):
        self.em.dma(B._a(out), B._a(in_), **kw)

    def fin(self):
        sem_cm = self.nc.semaphore()
        sem = sem_cm.__enter__()
        self.em.build(self.nc, sem)
        sem_cm.__exit__(None, None, None)
        self.stack.close()
        return self.nc


def ln_tile(b, src, ncols, cols, out, wbc=None, bbc=None, act=None, xc=None, sq=None):
    mu, vs, t0, rstd = cols['mu'], cols['vs'], cols['t0'], cols['rstd']
    b.red(mu, src, AL.add)
    b.ts(mu, mu, 1.0 / ncols, None, AL.mult)
    b.ts(xc, src, mu[:, 0:1], None, AL.subtract)
    b.act(sq, xc, AF.Square, accum=vs)
    b.ts(t0, vs, 1.0 / ncols, 1e-5, AL.mult, AL.add)
    b.act(t0, t0, AF.Ln)
    b.act(rstd, t0, AF.Exp, scale=-0.5)
    if wbc is None:
        b.ts(out, xc, rstd[:, 0:1], None, AL.mult)
    else:
        b.ts(xc, xc, rstd[:, 0:1], None, AL.mult)
        b.tt(xc, xc, wbc, AL.mult)
        if act is None:
            b.tt(out, xc, bbc, AL.add)
        else:
            b.tt(xc, xc, bbc, AL.add)
            b.act(out, xc, act)


def mk_cols(b, P=128):
    return dict(mu=b.sb([P, 1]), vs=b.sb([P, 1]), t0=b.sb([P, 1]), rstd=b.sb([P, 1]))


def bcast_row(b, ps_bc, row_sb, onesf, dst, n):
    for o in range(0, n, 512):
        w = min(512, n - o)
        b.mm(ps_bc[:, 0:w], onesf[0:1, :], row_sb[0:1, o:o + w], True, True)
        b.vcopy(dst[:, o:o + w], ps_bc[:, 0:w])


def build_ffn(E, R, topk, lora_scale, gate_bias):
    ER = E * R
    b = B()
    xin = b.inp("xin", [TOK, D])
    wT = b.inp("wT", [D, D])
    gwT = b.inp("gwT", [D, E])
    acatT = b.inp("acatT", [D, ER])
    bcatT = b.inp("bcatT", [ER, D])
    eb = b.inp("eb", [1, E])
    bb = b.inp("bb", [1, D])
    identb = b.inp("identb", [128, 128], BF16)
    ones1f = b.inp("ones1f", [1, 128])
    ones1b = b.inp("ones1b", [1, 128], BF16)
    xout = b.outp("xout", [TOK, D])

    uT = b.sb([128, KD * TOK], BF16)
    xt = b.sb([128, D])
    xc = b.sb([128, D])
    sq = b.sb([128, D])
    u_bf = b.sb([128, D], BF16)
    cols = mk_cols(b)
    idb = b.sb([128, 128], BF16)
    o1f = b.sb([1, 128])
    o1b = b.sb([1, 128], BF16)
    gw_bf = b.sb([128, KD * E], BF16)
    acat_bf = b.sb([128, KD * ER], BF16)
    bcat_bf = b.sb([ER, D], BF16)
    eb_bf = b.sb([1, E], BF16)
    bb_bf = b.sb([1, D], BF16)
    tmp_row = b.sb([1, D])
    logits = b.sb([128, E])
    m1 = b.sb([128, 1])
    m2 = b.sb([128, 1])
    s1 = b.sb([128, 1])
    s2 = b.sb([128, 1])
    oh1 = b.sb([128, E])
    oh2 = b.sb([128, E])
    msk = b.sb([128, E])
    wfull = b.sb([128, E])
    zw = b.sb([128, ER], BF16)
    zwT = b.sb([ER, TOK], BF16)
    wbuf = b.sb([128, 512])
    w_bf = b.sb([128, 512], BF16)
    xres = b.sb([128, 512])
    o_sb = b.sb([128, 512])

    tr_ps = b.ps([128, 128], BF16)
    sm_ps = b.ps([128, 128])
    mains = [b.ps([128, 512]) for _ in range(4)]

    b.dma(idb[:], identb)
    b.dma(o1f[:], ones1f)
    b.dma(o1b[:], ones1b)
    tmp_gw = b.sb([128, KD * E])
    b.dma(tmp_gw[:].rearrange("p (k e) -> p k e", k=KD), gwT.rearrange("(k p) e -> p k e", p=128))
    b.scopy(gw_bf[:], tmp_gw[:])
    tmp_ac = b.sb([128, KD * ER])
    b.dma(tmp_ac[:].rearrange("p (k e) -> p k e", k=KD), acatT.rearrange("(k p) e -> p k e", p=128))
    b.scopy(acat_bf[:], tmp_ac[:])
    tmp_bc = b.sb([ER, D])
    b.dma(tmp_bc[:], bcatT)
    b.scopy(bcat_bf[:], tmp_bc[:])
    tmp_eb = b.sb([1, E])
    b.dma(tmp_eb[:], eb)
    b.scopy(eb_bf[:], tmp_eb[:])
    b.dma(tmp_row[:], bb)
    b.scopy(bb_bf[:], tmp_row[:])

    for t in range(NT):
        b.dma(xt[:], xin[t * 128:(t + 1) * 128, :])
        ln_tile(b, xt[:], D, cols, u_bf[:], xc=xc[:], sq=sq[:])
        for k in range(KD):
            b.tr(tr_ps[:], u_bf[:, k * 128:(k + 1) * 128], idb[:])
            b.vcopy(uT[:, k * TOK + t * 128: k * TOK + (t + 1) * 128], tr_ps[:])

    def lhs(k, t):
        return uT[:, k * TOK + t * 128: k * TOK + (t + 1) * 128]

    for t in range(NT):
        for k in range(KD):
            b.mm(sm_ps[:, 0:E], lhs(k, t), gw_bf[:, k * E:(k + 1) * E], k == 0, False)
        b.mm(sm_ps[:, 0:E], o1b[0:1, :], eb_bf[0:1, :], False, True)
        b.vcopy(logits[:], sm_ps[:, 0:E])
        b.red(m1, logits[:], AL.max)
        b.ts(oh1[:], logits[:], m1[:, 0:1], None, AL.is_ge)
        if topk == 1:
            b.vcopy(wfull[:], oh1[:])
        else:
            b.stt(msk[:], oh1[:], -1e30, logits[:], AL.mult, AL.add)
            b.red(m2, msk[:], AL.max)
            b.ts(oh2[:], msk[:], m2[:, 0:1], None, AL.is_ge)
            b.ts(s1, m1[:], m2[:, 0:1], None, AL.subtract)
            b.act(s2, s1, AF.Sigmoid, scale=-1.0)
            b.act(s1, s1, AF.Sigmoid)
            b.ts(wfull[:], oh1[:], s1[:, 0:1], None, AL.mult)
            b.stt(wfull[:], oh2[:], s2[:, 0:1], wfull[:], AL.mult, AL.add)
        if lora_scale != 1.0:
            b.ts(wfull[:], wfull[:], float(lora_scale), None, AL.mult)
        for k in range(KD):
            b.mm(sm_ps[:, 0:ER], lhs(k, t), acat_bf[:, k * ER:(k + 1) * ER], k == 0, k == KD - 1)
        b.tt(zw[:].rearrange("p (e r) -> p e r", e=E),
             sm_ps[:, 0:ER].rearrange("p (e r) -> p e r", e=E),
             wfull[:].unsqueeze(2).broadcast_to([128, E, R]), AL.mult)
        b.tr(tr_ps[0:ER, :], zw[:], idb[:])
        b.vcopy(zwT[:, t * 128:(t + 1) * 128], tr_ps[0:ER, :])

    for tg in range(2):
        tl = list(range(tg * 4, tg * 4 + 4))
        for ob in range(8):
            for k in range(KD):
                b.dma(wbuf[:], wT[k * 128:(k + 1) * 128, ob * 512:(ob + 1) * 512])
                b.scopy(w_bf[:], wbuf[:])
                for i, t in enumerate(tl):
                    b.mm(mains[i][:], lhs(k, t), w_bf[:], k == 0, False)
            for i, t in enumerate(tl):
                b.mm(mains[i][:], zwT[:, t * 128:(t + 1) * 128], bcat_bf[:, ob * 512:(ob + 1) * 512], False, False)
                b.mm(mains[i][:], o1b[0:1, :], bb_bf[0:1, ob * 512:(ob + 1) * 512], False, True)
            for i, t in enumerate(tl):
                b.dma(xres[:], xin[t * 128:(t + 1) * 128, ob * 512:(ob + 1) * 512])
                b.stt(o_sb[:], mains[i][:], 0.5, xres[:], AL.mult, AL.add)
                b.dma(xout[t * 128:(t + 1) * 128, ob * 512:(ob + 1) * 512], o_sb[:])
    return b.fin()


def build_attn0_in():
    DI = 512
    b = B()
    xin = b.inp("xin", [TOK, D])
    winT = b.inp("winT", [D, DI])
    binr = b.inp("binr", [1, DI])
    ln1w = b.inp("ln1w", [1, DI])
    ln1b = b.inp("ln1b", [1, DI])
    identb = b.inp("identb", [128, 128], BF16)
    ones1f = b.inp("ones1f", [1, 128])
    ones1b = b.inp("ones1b", [1, 128], BF16)
    h0 = b.outp("h0", [TOK, DI])
    y0 = b.outp("y0", [TOK, DI])

    uT = b.sb([128, KD * TOK], BF16)
    xt = b.sb([128, D])
    xc = b.sb([128, D])
    sq = b.sb([128, D])
    u_bf = b.sb([128, D], BF16)
    cols = mk_cols(b)
    idb = b.sb([128, 128], BF16)
    o1f = b.sb([1, 128])
    o1b = b.sb([1, 128], BF16)
    bin_bf = b.sb([1, DI], BF16)
    tmp_row = b.sb([1, DI])
    wbuf = b.sb([128, 512])
    w_bf = b.sb([128, 512], BF16)
    wbc = b.sb([128, DI])
    bbc = b.sb([128, DI])
    h_sb = [b.sb([128, DI]) for _ in range(NT)]
    y_sb = b.sb([128, DI])
    xc2 = b.sb([128, DI])
    sq2 = b.sb([128, DI])

    tr_ps = b.ps([128, 128], BF16)
    bc_ps = b.ps([128, 512])
    mains = [b.ps([128, 512]) for _ in range(4)]

    b.dma(idb[:], identb)
    b.dma(o1f[:], ones1f)
    b.dma(o1b[:], ones1b)
    b.dma(tmp_row[:], binr)
    b.scopy(bin_bf[:], tmp_row[:])
    b.dma(tmp_row[:], ln1w)
    bcast_row(b, bc_ps, tmp_row, o1f, wbc, DI)
    b.dma(tmp_row[:], ln1b)
    bcast_row(b, bc_ps, tmp_row, o1f, bbc, DI)

    for t in range(NT):
        b.dma(xt[:], xin[t * 128:(t + 1) * 128, :])
        ln_tile(b, xt[:], D, cols, u_bf[:], xc=xc[:], sq=sq[:])
        for k in range(KD):
            b.tr(tr_ps[:], u_bf[:, k * 128:(k + 1) * 128], idb[:])
            b.vcopy(uT[:, k * TOK + t * 128: k * TOK + (t + 1) * 128], tr_ps[:])

    for tg in range(2):
        tl = list(range(tg * 4, tg * 4 + 4))
        for k in range(KD):
            b.dma(wbuf[:], winT[k * 128:(k + 1) * 128, :])
            b.scopy(w_bf[:], wbuf[:])
            for i, t in enumerate(tl):
                b.mm(mains[i][:], uT[:, k * TOK + t * 128: k * TOK + (t + 1) * 128], w_bf[:], k == 0, False)
        for i, t in enumerate(tl):
            b.mm(mains[i][:], o1b[0:1, :], bin_bf[0:1, :], False, True)
            b.vcopy(h_sb[t][:], mains[i][:])
            b.dma(h0[t * 128:(t + 1) * 128, :], h_sb[t][:])
    for t in range(NT):
        ln_tile(b, h_sb[t][:], DI, cols, y_sb[:], wbc=wbc[:], bbc=bbc[:], act=AF.Silu,
                xc=xc2[:], sq=sq2[:])
        b.dma(y0[t * 128:(t + 1) * 128, :], y_sb[:])
    return b.fin()


def build_scan():
    S = 4096
    b = B()
    ycs = b.inp("ycs", [S, 128])
    phr = b.inp("phr", [128, 1])
    phi = b.inp("phi", [128, 1])
    lcr = b.inp("lcr", [128, 1])
    lci = b.inp("lci", [128, 1])
    identf = b.inp("identf", [128, 128])
    convo = b.outp("convo", [S, 128])

    y_sb = b.sb([128, S])
    yT = b.sb([128, S])
    C = b.sb([128, S])
    Sn = b.sb([128, S])
    A = b.sb([128, S])
    Bb = b.sb([128, S])
    RHO = b.sb([128, S])
    tmp1 = b.sb([128, 2048])
    tmp2 = b.sb([128, 2048])
    idf = b.sb([128, 128])
    pr = b.sb([128, 1])
    pi = b.sb([128, 1])
    r2 = b.sb([128, 1])
    rr = b.sb([128, 1])
    th = b.sb([128, 1])
    ir = b.sb([128, 1])
    ere = b.sb([128, 1])
    eim = b.sb([128, 1])
    cn = b.sb([128, 1])
    sn = b.sb([128, 1])
    t1c = b.sb([128, 1])
    t2c = b.sb([128, 1])
    a0 = b.sb([128, 1])
    b0 = b.sb([128, 1])
    tr_ps = b.ps([128, 128])

    b.dma(idf[:], identf)
    b.dma(y_sb[:].rearrange("p (t c) -> p t c", t=KD), ycs.rearrange("(t p) c -> p t c", p=128))
    b.dma(pr[:], phr)
    b.dma(pi[:], phi)
    b.dma(a0[:], lcr)
    b.dma(b0[:], lci)
    for t in range(KD):
        b.tr(tr_ps[:], y_sb[:, t * 128:(t + 1) * 128], idf[:])
        b.vcopy(yT[:, t * 128:(t + 1) * 128], tr_ps[:])

    b.tt(r2[:], pr[:], pr[:], AL.mult)
    b.tt(t1c[:], pi[:], pi[:], AL.mult)
    b.tt(r2[:], r2[:], t1c[:], AL.add)
    b.act(rr, r2, AF.Sqrt)
    b.act(th, rr, AF.Tanh)
    b.recip(ir[:], rr[:])
    b.tt(ere[:], pr[:], ir[:], AL.mult)
    b.tt(eim[:], pi[:], ir[:], AL.mult)

    b.memset(C[:, 0:1], 1.0)
    b.memset(Sn[:, 0:1], 0.0)
    b.vcopy(cn[:], ere[:])
    b.vcopy(sn[:], eim[:])
    n = 1
    while n < S:
        if n > 1:
            b.tt(t1c[:], C[:, n - 1:n], ere[:], AL.mult)
            b.tt(t2c[:], Sn[:, n - 1:n], eim[:], AL.mult)
            b.tt(cn[:], t1c[:], t2c[:], AL.subtract)
            b.tt(t1c[:], C[:, n - 1:n], eim[:], AL.mult)
            b.tt(t2c[:], Sn[:, n - 1:n], ere[:], AL.mult)
            b.tt(sn[:], t1c[:], t2c[:], AL.add)
        m = min(n, S - n)
        b.ts(tmp1[:, 0:m], C[:, 0:m], cn[:, 0:1], None, AL.mult)
        b.ts(tmp2[:, 0:m], Sn[:, 0:m], sn[:, 0:1], None, AL.mult)
        b.tt(C[:, n:n + m], tmp1[:, 0:m], tmp2[:, 0:m], AL.subtract)
        b.ts(tmp1[:, 0:m], Sn[:, 0:m], cn[:, 0:1], None, AL.mult)
        b.ts(tmp2[:, 0:m], C[:, 0:m], sn[:, 0:1], None, AL.mult)
        b.tt(Sn[:, n:n + m], tmp1[:, 0:m], tmp2[:, 0:m], AL.add)
        n *= 2

    b.tt(t1c[:], a0[:], ere[:], AL.mult)
    b.tt(t2c[:], b0[:], eim[:], AL.mult)
    b.tt(r2[:], a0[:], eim[:], AL.mult)
    b.tt(ir[:], b0[:], ere[:], AL.mult)
    b.tt(a0[:], t1c[:], t2c[:], AL.subtract)
    b.tt(b0[:], r2[:], ir[:], AL.add)
    b.ts(b0[:], b0[:], -1.0, None, AL.mult)

    b.memset(RHO[:], 1.0)
    b.ts(RHO[:], RHO[:], th[:, 0:1], None, AL.mult)

    half = 2048
    b.tt(y_sb[:], yT[:], C[:], AL.mult)
    b.scan(A[:, 0:half], RHO[:, 0:half], y_sb[:, 0:half], a0[:, 0:1])
    b.scan(A[:, half:S], RHO[:, half:S], y_sb[:, half:S], A[:, half - 1:half])
    b.tt(y_sb[:], yT[:], Sn[:], AL.mult)
    b.scan(Bb[:, 0:half], RHO[:, 0:half], y_sb[:, 0:half], b0[:, 0:1])
    b.scan(Bb[:, half:S], RHO[:, half:S], y_sb[:, half:S], Bb[:, half - 1:half])

    b.tt(A[:], C[:], A[:], AL.mult)
    b.tt(Bb[:], Sn[:], Bb[:], AL.mult)
    b.tt(A[:], A[:], Bb[:], AL.add)

    for t in range(KD):
        b.tr(tr_ps[:], A[:, t * 128:(t + 1) * 128], idf[:])
        b.vcopy(y_sb[:, t * 128:(t + 1) * 128], tr_ps[:])
    b.dma(convo.rearrange("(t p) c -> p t c", p=128), y_sb[:].rearrange("p (t c) -> p t c", t=KD))
    return b.fin()


def build_mlp():
    DI, DF = 512, 2048
    b = B()
    hin = b.inp("hin", [TOK, DI])
    cin = b.inp("cin", [TOK, DI])
    sclr = b.inp("sclr", [1, DI])
    ln2w = b.inp("ln2w", [1, DI])
    ln2b = b.inp("ln2b", [1, DI])
    w1T = b.inp("w1T", [DI, DF])
    b1r = b.inp("b1r", [1, DF])
    w2T = b.inp("w2T", [DF, DI])
    b2r = b.inp("b2r", [1, DI])
    lnnw = b.inp("lnnw", [1, DI])
    lnnb = b.inp("lnnb", [1, DI])
    identb = b.inp("identb", [128, 128], BF16)
    ones1f = b.inp("ones1f", [1, 128])
    ones1b = b.inp("ones1b", [1, 128], BF16)
    hout = b.outp("hout", [TOK, DI])
    ynext = b.outp("ynext", [TOK, DI])

    idb = b.sb([128, 128], BF16)
    o1f = b.sb([1, 128])
    o1b = b.sb([1, 128], BF16)
    cols = mk_cols(b)
    h_t = b.sb([128, DI])
    c_t = b.sb([128, DI])
    xc = b.sb([128, DF])
    sq = b.sb([128, DF])
    sclbc = b.sb([128, DI])
    w2bc = b.sb([128, DI])
    b2bc = b.sb([128, DI])
    wnbc = b.sb([128, DI])
    bnbc = b.sb([128, DI])
    tmp_row = b.sb([1, DF])
    z_bf = b.sb([128, DI], BF16)
    zT = b.sb([128, DI], BF16)
    w1_sb = b.sb([128, 4 * DF])
    w1_bf = b.sb([128, 4 * DF], BF16)
    w2_sb = b.sb([128, 16 * DI])
    w2_bf = b.sb([128, 16 * DI], BF16)
    b1_bf = b.sb([1, DF], BF16)
    b2_bf = b.sb([1, DI], BF16)
    f_bf = b.sb([128, DF], BF16)
    fT = b.sb([128, DF], BF16)
    y_sb = b.sb([128, DI])

    tr_ps = b.ps([128, 128], BF16)
    bc_ps = b.ps([128, 512])
    mainA = b.ps([128, 512])
    mainB = b.ps([128, 512])

    b.dma(idb[:], identb)
    b.dma(o1f[:], ones1f)
    b.dma(o1b[:], ones1b)
    for (row, dst) in [(sclr, sclbc), (ln2w, w2bc), (ln2b, b2bc), (lnnw, wnbc), (lnnb, bnbc)]:
        b.dma(tmp_row[0:1, 0:DI], row)
        bcast_row(b, bc_ps, tmp_row[0:1, 0:DI], o1f, dst, DI)
    b.dma(w1_sb[:].rearrange("p (k f) -> p k f", k=4), w1T.rearrange("(k p) f -> p k f", p=128))
    b.scopy(w1_bf[:], w1_sb[:])
    b.dma(w2_sb[:].rearrange("p (k f) -> p k f", k=16), w2T.rearrange("(k p) f -> p k f", p=128))
    b.scopy(w2_bf[:], w2_sb[:])
    b.dma(tmp_row[:], b1r)
    b.scopy(b1_bf[:], tmp_row[:])
    b.dma(tmp_row[0:1, 0:DI], b2r)
    b.scopy(b2_bf[:], tmp_row[0:1, 0:DI])

    for t in range(NT):
        b.dma(h_t[:], hin[t * 128:(t + 1) * 128, :])
        b.dma(c_t[:], cin[t * 128:(t + 1) * 128, :])
        b.tt(c_t[:], c_t[:], sclbc[:], AL.mult)
        b.tt(h_t[:], h_t[:], c_t[:], AL.add)
        ln_tile(b, h_t[:], DI, cols, z_bf[:], wbc=w2bc[:], bbc=b2bc[:],
                xc=xc[:, 0:DI], sq=sq[:, 0:DI])
        for k in range(4):
            b.tr(tr_ps[:], z_bf[:, k * 128:(k + 1) * 128], idb[:])
            b.vcopy(zT[:, k * 128:(k + 1) * 128], tr_ps[:])
        for o in range(4):
            for k in range(4):
                b.mm(mainA[:], zT[:, k * 128:(k + 1) * 128],
                     w1_bf[:, k * DF + o * 512: k * DF + (o + 1) * 512], k == 0, False)
            b.mm(mainA[:], o1b[0:1, :], b1_bf[0:1, o * 512:(o + 1) * 512], False, True)
            b.act(f_bf[:, o * 512:(o + 1) * 512], mainA[:], AF.Silu)
        for k in range(16):
            b.tr(tr_ps[:], f_bf[:, k * 128:(k + 1) * 128], idb[:])
            b.vcopy(fT[:, k * 128:(k + 1) * 128], tr_ps[:])
        for k in range(16):
            b.mm(mainB[:], fT[:, k * 128:(k + 1) * 128],
                 w2_bf[:, k * DI:(k + 1) * DI], k == 0, False)
        b.mm(mainB[:], o1b[0:1, :], b2_bf[0:1, :], False, True)
        b.tt(h_t[:], h_t[:], mainB[:], AL.add)
        b.dma(hout[t * 128:(t + 1) * 128, :], h_t[:])
        ln_tile(b, h_t[:], DI, cols, y_sb[:], wbc=wnbc[:], bbc=bnbc[:], act=AF.Silu,
                xc=xc[:, 0:DI], sq=sq[:, 0:DI])
        b.dma(ynext[t * 128:(t + 1) * 128, :], y_sb[:])
    return b.fin()


def build_attn0_final():
    DI = 512
    b = B()
    hin = b.inp("hin", [TOK, DI])
    x1 = b.inp("x1", [TOK, D])
    lnlw = b.inp("lnlw", [1, DI])
    lnlb = b.inp("lnlb", [1, DI])
    woutT = b.inp("woutT", [DI, D])
    boutr = b.inp("boutr", [1, D])
    identb = b.inp("identb", [128, 128], BF16)
    ones1f = b.inp("ones1f", [1, 128])
    ones1b = b.inp("ones1b", [1, 128], BF16)
    x2 = b.outp("x2", [TOK, D])

    idb = b.sb([128, 128], BF16)
    o1f = b.sb([1, 128])
    o1b = b.sb([1, 128], BF16)
    cols = mk_cols(b)
    h_t = b.sb([128, DI])
    xc = b.sb([128, DI])
    sq = b.sb([128, DI])
    wbc = b.sb([128, DI])
    bbc = b.sb([128, DI])
    tmp_row = b.sb([1, D])
    hf_bf = b.sb([128, DI], BF16)
    hT = b.sb([128, DI], BF16)
    bout_bf = b.sb([1, D], BF16)
    xres = b.sb([128, 512])
    o_sb = b.sb([128, 512])

    tr_ps = b.ps([128, 128], BF16)
    bc_ps = b.ps([128, 512])
    main = b.ps([128, 512])

    b.dma(idb[:], identb)
    b.dma(o1f[:], ones1f)
    b.dma(o1b[:], ones1b)
    b.dma(tmp_row[0:1, 0:DI], lnlw)
    bcast_row(b, bc_ps, tmp_row[0:1, 0:DI], o1f, wbc, DI)
    b.dma(tmp_row[0:1, 0:DI], lnlb)
    bcast_row(b, bc_ps, tmp_row[0:1, 0:DI], o1f, bbc, DI)
    b.dma(tmp_row[:], boutr)
    b.scopy(bout_bf[:], tmp_row[:])
    w_f32 = b.sb([128, D])
    wout_bf = b.sb([128, 4 * D], BF16)
    for k in range(4):
        b.dma(w_f32[:], woutT[k * 128:(k + 1) * 128, :])
        b.scopy(wout_bf[:, k * D:(k + 1) * D], w_f32[:])

    for t in range(NT):
        b.dma(h_t[:], hin[t * 128:(t + 1) * 128, :])
        ln_tile(b, h_t[:], DI, cols, hf_bf[:], wbc=wbc[:], bbc=bbc[:], xc=xc[:], sq=sq[:])
        for k in range(4):
            b.tr(tr_ps[:], hf_bf[:, k * 128:(k + 1) * 128], idb[:])
            b.vcopy(hT[:, k * 128:(k + 1) * 128], tr_ps[:])
        for ob in range(8):
            for k in range(4):
                b.mm(main[:], hT[:, k * 128:(k + 1) * 128],
                     wout_bf[:, k * D + ob * 512: k * D + (ob + 1) * 512], k == 0, False)
            b.mm(main[:], o1b[0:1, :], bout_bf[0:1, ob * 512:(ob + 1) * 512], False, True)
            b.dma(xres[:], x1[t * 128:(t + 1) * 128, ob * 512:(ob + 1) * 512])
            b.tt(o_sb[:], main[:], xres[:], AL.add)
            b.dma(x2[t * 128:(t + 1) * 128, ob * 512:(ob + 1) * 512], o_sb[:])
    return b.fin()


def build_rwkv_proj():
    b = B()
    xin = b.inp("xin", [TOK, D])
    halo = b.inp("halo", [1, D])
    wts = [b.inp(nm, [D, D]) for nm in ["wkT", "wvT", "wrT", "wgT"]]
    tms = [b.inp(nm, [128, KD]) for nm in ["tmk", "tmv", "tmr", "tmg"]]
    identf = b.inp("identf", [128, 128])
    outs = [b.outp(nm, [TOK, D]) for nm in ["ko", "vo", "ro", "go"]]
    hscr = b.scr("hscr", [1, D])

    idf = b.sb([128, 128])
    cols = mk_cols(b)
    xt = b.sb([128, D])
    xc = b.sb([128, D])
    uTall = b.sb([128, KD * (TOK + 1)], BF16)
    mixT = b.sb([128, KD * TOK], BF16)
    tm_sb = [b.sb([128, KD]) for _ in range(4)]
    om_sb = [b.sb([128, KD]) for _ in range(4)]
    halocols = b.sb([128, KD])
    hcols = mk_cols(b, 1)
    wbuf = b.sb([128, 512])
    w_bf = b.sb([128, 512], BF16)
    o_sb = b.sb([128, 512])
    mtmp = b.sb([128, 128])

    mains = [b.ps([128, 512]) for _ in range(8)]

    b.dma(idf[:], identf)
    for m in range(4):
        b.dma(tm_sb[m][:], tms[m])
        b.ts(om_sb[m][:], tm_sb[m][:], -1.0, 1.0, AL.mult, AL.add)

    b.dma(xt[0:1, :], halo)
    ln_tile(b, xt[0:1, :], D, hcols, xc[0:1, :], xc=xc[0:1, :], sq=xt[0:1, :])
    b.dma(hscr, xc[0:1, :])
    b.dma(halocols[:], hscr.rearrange("o (f p) -> (o p) f", p=128),
          allow_slow_non_contiguous=True)
    for k in range(KD):
        b.scopy(uTall[:, k * (TOK + 1):k * (TOK + 1) + 1], halocols[:, k:k + 1])

    base = lambda k: k * (TOK + 1)
    for t in range(NT):
        b.dma(xt[:], xin[t * 128:(t + 1) * 128, :])
        ln_tile(b, xt[:], D, cols, xc[:], xc=xc[:], sq=xt[:])
        for k in range(KD):
            b.tr(mains[0][:, 0:128], xc[:, k * 128:(k + 1) * 128], idf[:])
            b.scopy(uTall[:, base(k) + 1 + t * 128: base(k) + 1 + (t + 1) * 128],
                    mains[0][:, 0:128])

    for m in range(4):
        for k in range(KD):
            for t in range(NT):
                cur = uTall[:, base(k) + 1 + t * 128: base(k) + 1 + (t + 1) * 128]
                sh = uTall[:, base(k) + t * 128: base(k) + (t + 1) * 128]
                b.ts(mtmp[:], sh, om_sb[m][:, k:k + 1], None, AL.mult)
                b.stt(mixT[:, k * TOK + t * 128: k * TOK + (t + 1) * 128],
                      cur, tm_sb[m][:, k:k + 1], mtmp[:], AL.mult, AL.add)
        for ob in range(8):
            for k in range(KD):
                b.dma(wbuf[:], wts[m][k * 128:(k + 1) * 128, ob * 512:(ob + 1) * 512])
                b.scopy(w_bf[:], wbuf[:])
                for t in range(NT):
                    b.mm(mains[t][:], mixT[:, k * TOK + t * 128: k * TOK + (t + 1) * 128],
                         w_bf[:], k == 0, k == KD - 1)
            for t in range(NT):
                if m == 3:
                    b.act(o_sb[:], mains[t][:], AF.Silu)
                else:
                    b.vcopy(o_sb[:], mains[t][:])
                b.dma(outs[m][t * 128:(t + 1) * 128, ob * 512:(ob + 1) * 512], o_sb[:])
    return b.fin()


def build_wkv():
    S, HD, NH, C = 4096, 64, 8, 128
    NJ = S // C
    b = B()
    kk = b.inp("kk", [2, S, NH * HD])
    vv = b.inp("vv", [2, S, NH * HD])
    rr = b.inp("rr", [2, S, NH * HD])
    td8 = b.inp("td8", [1, NH * HD])
    tf8 = b.inp("tf8", [1, NH * HD])
    wexp = b.inp("wexp", [1, 6 * 128])
    offpad = b.inp("offpad", [1, 784])
    mkeep = b.inp("mkeep", [128, 128])
    m16off = b.inp("m16off", [128, 128])
    mdiag = b.inp("mdiag", [128, 128])
    identf = b.inp("identf", [128, 128])
    ones1f = b.inp("ones1f", [1, 128])
    wo = b.outp("wo", [2, S, NH * HD])

    idf = b.sb([128, 128])
    o1f = b.sb([1, 128])
    exp_sb = b.sb([1, 6 * 128])
    off_sb = b.sb([1, 784])
    mk_sb = b.sb([128, 128])
    m16_sb = b.sb([128, 128])
    md_sb = b.sb([128, 128])
    td_sb = b.sb([1, NH * HD])
    tf_sb = b.sb([1, NH * HD])
    lnw = b.sb([1, NH * HD])

    DPOWi = [b.sb([HD, 128]) for _ in range(NH)]
    DPOW8 = [b.sb([HD, 128]) for _ in range(NH)]
    DPOW8K = [b.sb([HD, 128]) for _ in range(NH)]
    DPOWm8 = [b.sb([HD, 128]) for _ in range(NH)]
    DPOWm16 = [b.sb([HD, 128]) for _ in range(NH)]
    DTF = [b.sb([HD, 128]) for _ in range(NH)]
    OFF16 = [b.sb([HD, 784]) for _ in range(NH)]
    SPOWrev = [b.sb([128, HD]) for _ in range(NH)]
    w128c = [b.sb([HD, 1]) for _ in range(NH)]

    r_sb = b.sb([128, HD])
    k_sb = b.sb([128, HD])
    v_sb = b.sb([128, HD])
    vbf = b.sb([128, HD], BF16)
    kvbf = b.sb([128, HD], BF16)
    kT_sb = b.sb([HD, 128])
    Rtr = b.sb([HD, 128], BF16)
    Rm8 = b.sb([HD, 128], BF16)
    Rm16 = b.sb([HD, 128], BF16)
    Rtf = b.sb([HD, 128], BF16)
    Rint = b.sb([HD, 128], BF16)
    Ktr = b.sb([HD, 128], BF16)
    Kpl = b.sb([HD, 128], BF16)
    KOFF = b.sb([HD, 784], BF16)
    A_bf = b.sb([128, 128], BF16)
    tmp_bf = b.sb([128, 128], BF16)
    S_sb = b.sb([HD, HD])
    S_bf = b.sb([HD, HD], BF16)
    o_sb = b.sb([128, HD])

    At = b.ps([128, 128])
    o_ps = b.ps([128, HD])
    B_ps = b.ps([HD, HD])
    trR = b.ps([HD, 128])
    trK = b.ps([HD, 128])

    b.dma(idf[:], identf)
    b.dma(o1f[:], ones1f)
    b.dma(exp_sb[:], wexp)
    b.dma(off_sb[:], offpad)
    b.dma(mk_sb[:], mkeep)
    b.dma(m16_sb[:], m16off)
    b.dma(md_sb[:], mdiag)
    b.dma(td_sb[:], td8)
    b.dma(tf_sb[:], tf8)
    b.act(lnw[:], td_sb[:], AF.Exp)
    b.ts(lnw[:], lnw[:], -1.0, None, AL.mult)

    for h in range(NH):
        lr = lnw[0:1, h * HD:(h + 1) * HD]
        for (dst, row) in [(DPOWi[h], exp_sb[0:1, 0:128]), (DPOW8[h], exp_sb[0:1, 256:384]),
                           (DPOW8K[h], exp_sb[0:1, 384:512]), (DPOWm8[h], exp_sb[0:1, 512:640]),
                           (DPOWm16[h], exp_sb[0:1, 640:768])]:
            b.mm(At[0:HD, 0:128], lr, row, True, True)
            b.act(dst[:], At[0:HD, 0:128], AF.Exp)
        for j in range(7):
            b.mm(At[0:HD, 0:112], lr, off_sb[0:1, j * 112:(j + 1) * 112], True, True)
            b.act(OFF16[h][:, j * 112:(j + 1) * 112], At[0:HD, 0:112], AF.Exp)
        b.mm(At[0:HD, 0:128], tf_sb[0:1, h * HD:(h + 1) * HD], o1f[0:1, :], True, True)
        b.vcopy(DTF[h][:], At[0:HD, 0:128])
        b.mm(At[0:128, 0:HD], exp_sb[0:1, 128:256], lr, True, True)
        b.act(SPOWrev[h][:], At[0:128, 0:HD], AF.Exp)
        b.mm(At[0:HD, 0:1], lr, o1f[0:1, 0:1], True, True)
        b.act(w128c[h][:], At[0:HD, 0:1], AF.Exp, scale=128.0)

    for bb in range(2):
        for h in range(NH):
            b.memset(S_sb[:], 0.0)
            for j in range(NJ):
                rows = slice(j * 128, (j + 1) * 128)
                colsl = slice(h * HD, (h + 1) * HD)
                b.dma(r_sb[:], rr[bb, rows, colsl])
                b.dma(k_sb[:], kk[bb, rows, colsl])
                b.dma(v_sb[:], vv[bb, rows, colsl])
                b.scopy(vbf[:], v_sb[:])
                b.tt(kvbf[:], k_sb[:], SPOWrev[h][:], AL.mult)
                b.tr(trR[:], r_sb[:], idf[:])
                b.tr(trK[:], k_sb[:], idf[:])
                b.tt(Rtr[:], trR[:], DPOW8[h][:], AL.mult)
                b.tt(Rm8[:], trR[:], DPOWm8[h][:], AL.mult)
                b.tt(Rm16[:], trR[:], DPOWm16[h][:], AL.mult)
                b.tt(Rtf[:], trR[:], DTF[h][:], AL.mult)
                b.tt(Rint[:], trR[:], DPOWi[h][:], AL.mult)
                b.tt(Ktr[:], trK[:], DPOW8K[h][:], AL.mult)
                b.scopy(Kpl[:], trK[:])
                b.scopy(kT_sb[:], trK[:])
                b.tt(KOFF[:].rearrange("d (j t) -> d j t", j=7),
                     kT_sb[:, 0:112].unsqueeze(1).broadcast_to([HD, 7, 112]),
                     OFF16[h][:].rearrange("d (j t) -> d j t", j=7), AL.mult)
                b.mm(At[:], Ktr[:], Rtr[:], True, True)
                for J in range(1, 8):
                    b.mm(At[0:16 * J, 16 * J:16 * J + 16],
                         KOFF[:, (J - 1) * 112:(J - 1) * 112 + 16 * J],
                         Rm16[:, 16 * J:16 * J + 16], True, True)
                b.tt(A_bf[:], At[:], mk_sb[:], AL.mult)
                b.mm(At[:], Ktr[:], Rm8[:], True, True)
                b.tt(tmp_bf[:], At[:], m16_sb[:], AL.mult)
                b.tt(A_bf[:], A_bf[:], tmp_bf[:], AL.add)
                b.mm(At[:], Kpl[:], Rtf[:], True, True)
                b.tt(tmp_bf[:], At[:], md_sb[:], AL.mult)
                b.tt(A_bf[:], A_bf[:], tmp_bf[:], AL.add)
                b.mm(o_ps[:], A_bf[:], vbf[:], True, j == 0)
                if j > 0:
                    b.mm(o_ps[:], Rint[:], S_bf[:], False, True)
                b.scopy(o_sb[:], o_ps[:])
                b.dma(wo[bb, rows, colsl], o_sb[:])
                b.mm(B_ps[:], kvbf[:], vbf[:], True, True)
                b.stt(S_sb[:], S_sb[:], w128c[h][:, 0:1], B_ps[:], AL.mult, AL.add)
                b.scopy(S_bf[:], S_sb[:])
    return b.fin()


def build_rwkv_out():
    b = B()
    wkvi = b.inp("wkvi", [TOK, D])
    gi = b.inp("gi", [TOK, D])
    x4 = b.inp("x4", [TOK, D])
    gnw = b.inp("gnw", [1, D])
    gnb = b.inp("gnb", [1, D])
    woT = b.inp("woT", [D, D])
    identb = b.inp("identb", [128, 128], BF16)
    ones1f = b.inp("ones1f", [1, 128])
    x5 = b.outp("x5", [TOK, D])

    idb = b.sb([128, 128], BF16)
    o1f = b.sb([1, 128])
    wk_t = b.sb([128, D])
    g_t = b.sb([128, D])
    xc = b.sb([128, D])
    sq = b.sb([128, D])
    mu_g = b.sb([128, 64])
    var_g = b.sb([128, 64])
    gnwbc = b.sb([128, D])
    gnbbc = b.sb([128, D])
    tmp_row = b.sb([1, D])
    y_bf = b.sb([128, D], BF16)
    yT = b.sb([128, KD * TOK], BF16)
    wbuf = b.sb([128, 512])
    w_bf = b.sb([128, 512], BF16)
    xres = b.sb([128, 512])
    o_sb = b.sb([128, 512])

    tr_ps = b.ps([128, 128], BF16)
    bc_ps = b.ps([128, 512])
    mains = [b.ps([128, 512]) for _ in range(4)]

    b.dma(idb[:], identb)
    b.dma(o1f[:], ones1f)
    b.dma(tmp_row[:], gnw)
    bcast_row(b, bc_ps, tmp_row, o1f, gnwbc, D)
    b.dma(tmp_row[:], gnb)
    bcast_row(b, bc_ps, tmp_row, o1f, gnbbc, D)

    r3 = lambda ap: ap.rearrange("p (g i) -> p g i", g=64)
    for t in range(NT):
        b.dma(wk_t[:], wkvi[t * 128:(t + 1) * 128, :])
        b.dma(g_t[:], gi[t * 128:(t + 1) * 128, :])
        b.red(mu_g[:], r3(wk_t[:]), AL.add)
        b.ts(mu_g[:], mu_g[:], 1.0 / 64, None, AL.mult)
        b.tt(r3(xc[:]), r3(wk_t[:]), mu_g[:].unsqueeze(2).broadcast_to([128, 64, 64]), AL.subtract)
        b.act(sq[:], xc[:], AF.Square)
        b.red(var_g[:], r3(sq[:]), AL.add)
        b.ts(var_g[:], var_g[:], 1.0 / 64, 1e-5, AL.mult, AL.add)
        b.act(var_g[:], var_g[:], AF.Ln)
        b.act(var_g[:], var_g[:], AF.Exp, scale=-0.5)
        b.tt(r3(xc[:]), r3(xc[:]), var_g[:].unsqueeze(2).broadcast_to([128, 64, 64]), AL.mult)
        b.tt(xc[:], xc[:], gnwbc[:], AL.mult)
        b.tt(xc[:], xc[:], gnbbc[:], AL.add)
        b.tt(y_bf[:], xc[:], g_t[:], AL.mult)
        for k in range(KD):
            b.tr(tr_ps[:], y_bf[:, k * 128:(k + 1) * 128], idb[:])
            b.vcopy(yT[:, k * TOK + t * 128: k * TOK + (t + 1) * 128], tr_ps[:])

    for tg in range(2):
        tl = list(range(tg * 4, tg * 4 + 4))
        for ob in range(8):
            for k in range(KD):
                b.dma(wbuf[:], woT[k * 128:(k + 1) * 128, ob * 512:(ob + 1) * 512])
                b.scopy(w_bf[:], wbuf[:])
                for i, t in enumerate(tl):
                    b.mm(mains[i][:], yT[:, k * TOK + t * 128: k * TOK + (t + 1) * 128],
                         w_bf[:], k == 0, k == KD - 1)
            for i, t in enumerate(tl):
                b.dma(xres[:], x4[t * 128:(t + 1) * 128, ob * 512:(ob + 1) * 512])
                b.tt(o_sb[:], mains[i][:], xres[:], AL.add)
                b.dma(x5[t * 128:(t + 1) * 128, ob * 512:(ob + 1) * 512], o_sb[:])
    return b.fin()


_cache = {}


def _get(name, fn):
    if name not in _cache:
        _cache[name] = fn()
    return _cache[name]


def _run(nc, in_maps):
    res = run_bass_kernel_spmd(nc, in_maps, core_ids=list(range(NCORE)))
    return res.results


def _consts():
    c = {}
    c['identb'] = np.eye(128).astype(BF)
    c['identf'] = np.eye(128, dtype=np.float32)
    c['ones1f'] = np.ones((1, 128), dtype=np.float32)
    c['ones1b'] = np.ones((1, 128)).astype(BF)
    return c


def _shards(x):
    return [np.ascontiguousarray(x[c * TOK:(c + 1) * TOK]) for c in range(NCORE)]


def _gather(res, key):
    return np.concatenate([np.asarray(r[key], dtype=np.float32) for r in res], axis=0)


def launch_ffn(x, wT, gwT, acatT, bcatT, eb, bb, cfg, consts):
    nc = _get(f"ffn{cfg}", lambda: build_ffn(*cfg))
    com = dict(wT=wT, gwT=gwT, acatT=acatT, bcatT=bcatT, eb=eb, bb=bb,
               identb=consts['identb'], ones1f=consts['ones1f'], ones1b=consts['ones1b'])
    maps = [dict(xin=s, **com) for s in _shards(x)]
    return _gather(_run(nc, maps), "xout")


def kernel(_debug_cb=None, **inp):
    _cb = _debug_cb if _debug_cb is not None else (lambda *a: None)
    consts = _consts()
    f32 = np.float32
    x = np.ascontiguousarray(np.asarray(inp["L0_attn_x"], f32).reshape(2 * 4096, D))

    l0_wT = np.ascontiguousarray(np.asarray(inp["L0_ffn_W_up"], f32).T)
    l0_gwT = np.ascontiguousarray(np.asarray(inp["L0_ffn_gate_w"], f32).T)
    l0_eb = np.ascontiguousarray(np.asarray(inp["L0_ffn_expert_bias"], f32).reshape(1, 4))
    l0_acatT = np.ascontiguousarray(np.asarray(inp["L0_ffn_lora_A"], f32).reshape(32, D).T)
    l0_bcatT = np.ascontiguousarray(
        np.asarray(inp["L0_ffn_lora_B"], f32).transpose(0, 2, 1).reshape(32, D))
    l0_bb = np.zeros((1, D), f32)
    cfg0 = (4, 8, 1, 1.0, True)

    l1_wT = np.ascontiguousarray(np.asarray(inp["L1_ffn_base_w"], f32).T)
    l1_gwT = np.ascontiguousarray(np.asarray(inp["L1_ffn_gate_w"], f32).T)
    l1_eb = np.zeros((1, 8), f32)
    l1_acatT = np.ascontiguousarray(np.asarray(inp["L1_ffn_lora_A"], f32).reshape(128, D).T)
    l1_bcatT = np.ascontiguousarray(
        np.asarray(inp["L1_ffn_lora_B"], f32).transpose(0, 2, 1).reshape(128, D))
    l1_bb = np.ascontiguousarray(np.asarray(inp["L1_ffn_base_b"], f32).reshape(1, D))
    cfg1 = (8, 16, 2, 2.0, False)

    x1 = launch_ffn(x, l0_wT, l0_gwT, l0_acatT, l0_bcatT, l0_eb, l0_bb, cfg0, consts)

    _cb("x1", x1)
    nc2 = _get("a0in", build_attn0_in)
    winT = np.ascontiguousarray(np.asarray(inp["L0_attn_Win"], f32).T)
    binr = np.ascontiguousarray(np.asarray(inp["L0_attn_bin"], f32).reshape(1, 512))
    ln1w = np.asarray(inp["L0_attn_ln1_w"], f32)
    ln1b = np.asarray(inp["L0_attn_ln1_b"], f32)
    com = dict(winT=winT, binr=binr,
               ln1w=np.ascontiguousarray(ln1w[0].reshape(1, 512)),
               ln1b=np.ascontiguousarray(ln1b[0].reshape(1, 512)),
               identb=consts['identb'], ones1f=consts['ones1f'], ones1b=consts['ones1b'])
    res = _run(nc2, [dict(xin=s, **com) for s in _shards(x1)])
    h = _gather(res, "h0")
    y = _gather(res, "y0")
    _cb("h0", h)
    _cb("y0", y)

    nc_scan = _get("scan", build_scan)
    nc_mlp = _get("mlp", build_mlp)
    phr = np.asarray(inp["L0_attn_phz_re"], f32)
    phi = np.asarray(inp["L0_attn_phz_im"], f32)
    lcr = np.asarray(inp["L0_attn_lc_re"], f32)
    lci = np.asarray(inp["L0_attn_lc_im"], f32)
    scl = np.asarray(inp["L0_attn_sc_lin"], f32)
    ln2w = np.asarray(inp["L0_attn_ln2_w"], f32)
    ln2b = np.asarray(inp["L0_attn_ln2_b"], f32)
    W1 = np.asarray(inp["L0_attn_W1"], f32)
    b1 = np.asarray(inp["L0_attn_b1"], f32)
    W2 = np.asarray(inp["L0_attn_W2"], f32)
    b2 = np.asarray(inp["L0_attn_b2"], f32)
    for t in range(4):
        maps = []
        for c in range(NCORE):
            bb_, cg = c // 4, c % 4
            sl = slice(cg * 128, (cg + 1) * 128)
            maps.append(dict(
                ycs=np.ascontiguousarray(y[bb_ * 4096:(bb_ + 1) * 4096, sl]),
                phr=np.ascontiguousarray(phr[t, sl].reshape(128, 1)),
                phi=np.ascontiguousarray(phi[t, sl].reshape(128, 1)),
                lcr=np.ascontiguousarray(lcr[t, sl].reshape(128, 1)),
                lci=np.ascontiguousarray(lci[t, sl].reshape(128, 1)),
                identf=consts['identf']))
        res = _run(nc_scan, maps)
        conv = np.zeros((8192, 512), f32)
        for c in range(NCORE):
            bb_, cg = c // 4, c % 4
            conv[bb_ * 4096:(bb_ + 1) * 4096, cg * 128:(cg + 1) * 128] = \
                np.asarray(res[c]["convo"], f32)
        tn = min(t + 1, 3)
        com = dict(sclr=np.ascontiguousarray(scl[t].reshape(1, 512)),
                   ln2w=np.ascontiguousarray(ln2w[t].reshape(1, 512)),
                   ln2b=np.ascontiguousarray(ln2b[t].reshape(1, 512)),
                   w1T=np.ascontiguousarray(W1[t].T), b1r=np.ascontiguousarray(b1[t].reshape(1, 2048)),
                   w2T=np.ascontiguousarray(W2[t].T), b2r=np.ascontiguousarray(b2[t].reshape(1, 512)),
                   lnnw=np.ascontiguousarray(ln1w[tn].reshape(1, 512)),
                   lnnb=np.ascontiguousarray(ln1b[tn].reshape(1, 512)),
                   identb=consts['identb'], ones1f=consts['ones1f'], ones1b=consts['ones1b'])
        hs = _shards(h)
        cs = _shards(conv)
        _cb(f"conv{t}", conv)
        res = _run(nc_mlp, [dict(hin=hs[c], cin=cs[c], **com) for c in range(NCORE)])
        h = _gather(res, "hout")
        y = _gather(res, "ynext")
        _cb(f"h{t + 1}", h)
        _cb(f"y{t + 1}", y)

    nc_f = _get("a0fin", build_attn0_final)
    com = dict(lnlw=np.ascontiguousarray(np.asarray(inp["L0_attn_lnl_w"], f32).reshape(1, 512)),
               lnlb=np.ascontiguousarray(np.asarray(inp["L0_attn_lnl_b"], f32).reshape(1, 512)),
               woutT=np.ascontiguousarray(np.asarray(inp["L0_attn_Wout"], f32).T),
               boutr=np.ascontiguousarray(np.asarray(inp["L0_attn_bout"], f32).reshape(1, D)),
               identb=consts['identb'], ones1f=consts['ones1f'], ones1b=consts['ones1b'])
    hs = _shards(h)
    x1s = _shards(x1)
    res = _run(nc_f, [dict(hin=hs[c], x1=x1s[c], **com) for c in range(NCORE)])
    x2 = _gather(res, "x2")
    _cb("x2", x2)

    x3 = launch_ffn(x2, l0_wT, l0_gwT, l0_acatT, l0_bcatT, l0_eb, l0_bb, cfg0, consts)
    _cb("x3", x3)
    x4 = launch_ffn(x3, l1_wT, l1_gwT, l1_acatT, l1_bcatT, l1_eb, l1_bb, cfg1, consts)
    _cb("x4", x4)

    nc_p = _get("rwkvp", build_rwkv_proj)
    w_tm = {}
    for nm, key in [("tmk", "L1_attn_time_mix_key"), ("tmv", "L1_attn_time_mix_value"),
                    ("tmr", "L1_attn_time_mix_receptance"), ("tmg", "L1_attn_time_mix_gate")]:
        w_tm[nm] = np.ascontiguousarray(np.asarray(inp[key], f32).reshape(KD, 128).T)
    com = dict(wkT=np.ascontiguousarray(np.asarray(inp["L1_attn_w_key"], f32).T),
               wvT=np.ascontiguousarray(np.asarray(inp["L1_attn_w_value"], f32).T),
               wrT=np.ascontiguousarray(np.asarray(inp["L1_attn_w_receptance"], f32).T),
               wgT=np.ascontiguousarray(np.asarray(inp["L1_attn_w_gate"], f32).T),
               identf=consts['identf'], **w_tm)
    x4s = _shards(x4)
    maps = []
    for c in range(NCORE):
        if c % 4 == 0:
            halo = np.zeros((1, D), f32)
        else:
            halo = np.ascontiguousarray(x4[c * TOK - 1:c * TOK, :])
        maps.append(dict(xin=x4s[c], halo=halo, **com))
    res = _run(nc_p, maps)
    kf = _gather(res, "ko")
    vf = _gather(res, "vo")
    rf = _gather(res, "ro")
    gf = _gather(res, "go")
    _cb("kf", kf)
    _cb("vf", vf)
    _cb("rf", rf)
    _cb("gf", gf)

    nc_w = _get("wkv", build_wkv)
    td = np.asarray(inp["L1_attn_time_decay"], f32)
    tf = np.asarray(inp["L1_attn_time_faaaa"], f32)
    t_ar = np.arange(128, dtype=f32)
    wexp = np.stack([t_ar, 127 - t_ar, (t_ar % 8) - 8, 7 - (t_ar % 8),
                     t_ar % 8, t_ar % 16]).astype(f32).reshape(1, 768)
    offpad = np.full((1, 784), 300.0, f32)
    for J in range(1, 8):
        tt_ = np.arange(16 * J, dtype=f32)
        offpad[0, (J - 1) * 112:(J - 1) * 112 + 16 * J] = 16 * J - 1 - tt_
    ti = np.arange(128)
    tgrid, igrid = np.meshgrid(ti, ti, indexing='ij')
    mkeep = ((tgrid < igrid) & ~((tgrid // 16 == igrid // 16) & (tgrid % 16 < 8)
                                 & (igrid % 16 >= 8))).astype(f32)
    m16off = ((tgrid // 16 == igrid // 16) & (tgrid % 16 < 8) & (igrid % 16 >= 8)).astype(f32)
    mdiag = np.eye(128, dtype=f32)
    k3 = kf.reshape(2, 4096, D)
    v3 = vf.reshape(2, 4096, D)
    r3_ = rf.reshape(2, 4096, D)
    maps = []
    for c in range(NCORE):
        sl = slice(c * 512, (c + 1) * 512)
        hsl = slice(c * 8, (c + 1) * 8)
        maps.append(dict(kk=np.ascontiguousarray(k3[:, :, sl]),
                         vv=np.ascontiguousarray(v3[:, :, sl]),
                         rr=np.ascontiguousarray(r3_[:, :, sl]),
                         td8=np.ascontiguousarray(td[hsl].reshape(1, 512)),
                         tf8=np.ascontiguousarray(tf[hsl].reshape(1, 512)),
                         wexp=wexp, offpad=offpad, mkeep=mkeep, m16off=m16off, mdiag=mdiag,
                         identf=consts['identf'], ones1f=consts['ones1f']))
    res = _run(nc_w, maps)
    wkv = np.zeros((2, 4096, D), f32)
    for c in range(NCORE):
        wkv[:, :, c * 512:(c + 1) * 512] = np.asarray(res[c]["wo"], f32)
    wkv = wkv.reshape(8192, D)
    _cb("wkv", wkv)

    nc_o = _get("rwkvo", build_rwkv_out)
    com = dict(gnw=np.ascontiguousarray(np.asarray(inp["L1_attn_gn_weight"], f32).reshape(1, D)),
               gnb=np.ascontiguousarray(np.asarray(inp["L1_attn_gn_bias"], f32).reshape(1, D)),
               woT=np.ascontiguousarray(np.asarray(inp["L1_attn_w_output"], f32).T),
               identb=consts['identb'], ones1f=consts['ones1f'])
    ws = _shards(wkv)
    gs = _shards(gf)
    res = _run(nc_o, [dict(wkvi=ws[c], gi=gs[c], x4=x4s[c], **com) for c in range(NCORE)])
    x5 = _gather(res, "x5")
    _cb("x5", x5)

    x6 = launch_ffn(x5, l1_wT, l1_gwT, l1_acatT, l1_bcatT, l1_eb, l1_bb, cfg1, consts)
    _cb("x6", x6)
    return x6.reshape(2, 4096, D)


# version 04
# speedup vs baseline: 1.4854x; 1.4854x over previous
import contextlib
import numpy as np
import ml_dtypes
import concourse.bass as bass
import concourse.mybir as mybir
from concourse.bass_utils import run_bass_kernel_spmd

F32 = mybir.dt.float32
BF16 = mybir.dt.bfloat16
AF = mybir.ActivationFunctionType
AL = mybir.AluOpType
AX = mybir.AxisListType.X
BF = ml_dtypes.bfloat16

NCORE = 8
TOK = 1024
NT = 8
D = 4096
KD = 32


class Em:
    def __init__(self):
        self.ops = []

    def op(self, eng, fn, inc=1):
        self.ops.append((eng, fn, inc))

    def v(self, fn):
        self.op('vector', fn, 1)

    def s(self, fn):
        self.op('scalar', fn, 1)

    def t(self, fn):
        self.op('tensor', fn, 1)

    def dma(self, out, in_, **kw):
        self.op('sync', lambda e, o=out, i=in_, k=kw: e.dma_start(o, i, **k), 16)

    def build(self, nc, sem):
        engines = ['sync', 'vector', 'scalar', 'tensor', 'gpsimd']
        n = len(self.ops)
        wait_flag = [True] * n
        inc_amt = [inc for (_, _, inc) in self.ops]
        i = 0
        while i < n:
            if self.ops[i][0] == 'tensor':
                j = i
                while j + 1 < n and self.ops[j + 1][0] == 'tensor':
                    j += 1
                if j > i:
                    tot = sum(inc_amt[i:j + 1])
                    for k in range(i, j + 1):
                        inc_amt[k] = 0
                        if k > i:
                            wait_flag[k] = False
                    inc_amt[j] = tot
                i = j + 1
            else:
                i += 1
        with nc.Block() as block:
            def mk(name):
                def f(eng):
                    cum = 0
                    for idx, (ename, fn, inc) in enumerate(self.ops):
                        if ename == name:
                            if cum > 0 and wait_flag[idx]:
                                eng.wait_ge(sem, cum)
                            inst = fn(eng)
                            if inc_amt[idx] > 0:
                                inst.then_inc(sem, inc_amt[idx])
                        cum += inc
                    eng.wait_ge(sem, cum)
                return f
            for name in engines:
                if any(e == name for (e, _, _) in self.ops):
                    getattr(block, name)(mk(name))


class B:
    def __init__(self):
        self.nc = bass.Bass("TRN2", target_bir_lowering=False)
        self.em = Em()
        self.stack = contextlib.ExitStack()
        self._n = 0

    def inp(self, name, shape, dt=F32):
        return self.nc.dram_tensor(name, list(shape), dt, kind="ExternalInput").ap()

    def outp(self, name, shape, dt=F32):
        return self.nc.dram_tensor(name, list(shape), dt, kind="ExternalOutput").ap()

    def scr(self, name, shape, dt=F32):
        return self.nc.dram_tensor(name, list(shape), dt, kind="Internal").ap()

    def sb(self, shape, dt=F32):
        self._n += 1
        return self.stack.enter_context(self.nc.sbuf_tensor(f"sb{self._n}", list(shape), dt))

    def ps(self, shape, dt=F32):
        self._n += 1
        return self.stack.enter_context(self.nc.psum_tensor(f"ps{self._n}", list(shape), dt))

    @staticmethod
    def _a(x):
        return x[:] if hasattr(x, 'alloc') or type(x).__name__ == 'SBTensorHandle' else x

    def tt(self, out, a, b_, op):
        out, a, b_ = B._a(out), B._a(a), B._a(b_)
        self.em.v(lambda e, o=out, x=a, y=b_, p=op: e.tensor_tensor(o, x, y, p))

    def ts(self, out, a, s1, s2, op0, op1=None):
        out, a = B._a(out), B._a(a)
        s1 = B._a(s1) if not isinstance(s1, (int, float)) else s1
        if s2 is not None and not isinstance(s2, (int, float)):
            s2 = B._a(s2)
        if op1 is None:
            self.em.v(lambda e, o=out, x=a, u=s1, p=op0: e.tensor_scalar(o, x, u, None, p))
        else:
            self.em.v(lambda e, o=out, x=a, u=s1, w=s2, p=op0, q=op1:
                      e.tensor_scalar(o, x, u, w, p, q))

    def stt(self, out, in0, scalar, in1, op0, op1):
        out, in0, in1 = B._a(out), B._a(in0), B._a(in1)
        if not isinstance(scalar, (int, float)):
            scalar = B._a(scalar)
        self.em.v(lambda e, o=out, x=in0, s=scalar, y=in1, p=op0, q=op1:
                  e.scalar_tensor_tensor(o, x, s, y, p, q))

    def red(self, out, a, op):
        out, a = B._a(out), B._a(a)
        self.em.v(lambda e, o=out, x=a, p=op: e.tensor_reduce(o, x, AX, p))

    def act(self, out, a, func, bias=None, scale=None, accum=None):
        out, a = B._a(out), B._a(a)
        if bias is not None and not isinstance(bias, (int, float)):
            bias = B._a(bias)
        if accum is not None:
            accum = B._a(accum)
        kw = {}
        if bias is not None:
            kw['bias'] = bias
        if scale is not None:
            kw['scale'] = scale
        if accum is not None:
            kw['accum_out'] = accum
        self.em.s(lambda e, o=out, x=a, f=func, k=kw: e.activation(o, x, f, **k))

    def vcopy(self, out, a):
        out, a = B._a(out), B._a(a)
        self.em.v(lambda e, o=out, x=a: e.tensor_copy(o, x))

    def scopy(self, out, a):
        out, a = B._a(out), B._a(a)
        self.em.s(lambda e, o=out, x=a: e.activation(o, x, AF.Copy))

    def memset(self, ap, c):
        ap = B._a(ap)
        self.em.v(lambda e, o=ap, v=c: e.memset(o, v))

    def recip(self, out, a):
        out, a = B._a(out), B._a(a)
        self.em.v(lambda e, o=out, x=a: e.reciprocal(o, x))

    def scan(self, out, d0, d1, init):
        out, d0, d1, init = B._a(out), B._a(d0), B._a(d1), B._a(init)
        self.em.v(lambda e, o=out, a=d0, b_=d1, i=init:
                  e.tensor_tensor_scan(o, a, b_, i, AL.mult, AL.add))

    def mm(self, out, lhsT, rhs, start, stop):
        out, lhsT, rhs = B._a(out), B._a(lhsT), B._a(rhs)
        self.em.t(lambda e, o=out, l=lhsT, r=rhs, s1=start, s2=stop:
                  e.matmul(o, l, r, start=s1, stop=s2))

    def tr(self, out, a, ident):
        out, a, ident = B._a(out), B._a(a), B._a(ident)
        self.em.t(lambda e, o=out, x=a, i=ident: e.transpose(o, x, i))

    def dma(self, out, in_, **kw):
        self.em.dma(B._a(out), B._a(in_), **kw)

    def fin(self):
        sem_cm = self.nc.semaphore()
        sem = sem_cm.__enter__()
        self.em.build(self.nc, sem)
        sem_cm.__exit__(None, None, None)
        self.stack.close()
        return self.nc


def ln_tile(b, src, ncols, cols, out, wbc=None, bbc=None, act=None, xc=None, sq=None):
    mu, vs, t0, rstd = cols['mu'], cols['vs'], cols['t0'], cols['rstd']
    b.red(mu, src, AL.add)
    b.ts(mu, mu, 1.0 / ncols, None, AL.mult)
    b.ts(xc, src, mu[:, 0:1], None, AL.subtract)
    b.act(sq, xc, AF.Square, accum=vs)
    b.ts(t0, vs, 1.0 / ncols, 1e-5, AL.mult, AL.add)
    b.act(t0, t0, AF.Ln)
    b.act(rstd, t0, AF.Exp, scale=-0.5)
    if wbc is None:
        b.ts(out, xc, rstd[:, 0:1], None, AL.mult)
    else:
        b.ts(xc, xc, rstd[:, 0:1], None, AL.mult)
        b.tt(xc, xc, wbc, AL.mult)
        if act is None:
            b.tt(out, xc, bbc, AL.add)
        else:
            b.tt(xc, xc, bbc, AL.add)
            b.act(out, xc, act)


def mk_cols(b, P=128):
    return dict(mu=b.sb([P, 1]), vs=b.sb([P, 1]), t0=b.sb([P, 1]), rstd=b.sb([P, 1]))


def bcast_row(b, ps_bc, row_sb, onesf, dst, n):
    for o in range(0, n, 512):
        w = min(512, n - o)
        b.mm(ps_bc[:, 0:w], onesf[0:1, :], row_sb[0:1, o:o + w], True, True)
        b.vcopy(dst[:, o:o + w], ps_bc[:, 0:w])


def build_ffn(E, R, topk, lora_scale, gate_bias):
    ER = E * R
    b = B()
    xin = b.inp("xin", [TOK, D])
    wT = b.inp("wT", [D, D])
    gwT = b.inp("gwT", [D, E])
    acatT = b.inp("acatT", [D, ER])
    bcatT = b.inp("bcatT", [ER, D])
    eb = b.inp("eb", [1, E])
    bb = b.inp("bb", [1, D])
    identb = b.inp("identb", [128, 128], BF16)
    ones1f = b.inp("ones1f", [1, 128])
    ones1b = b.inp("ones1b", [1, 128], BF16)
    xout = b.outp("xout", [TOK, D])

    uT = b.sb([128, KD * TOK], BF16)
    xt = b.sb([128, D])
    xc = b.sb([128, D])
    sq = b.sb([128, D])
    u_bf = b.sb([128, D], BF16)
    cols = mk_cols(b)
    idb = b.sb([128, 128], BF16)
    o1f = b.sb([1, 128])
    o1b = b.sb([1, 128], BF16)
    gw_bf = b.sb([128, KD * E], BF16)
    acat_bf = b.sb([128, KD * ER], BF16)
    bcat_bf = b.sb([ER, D], BF16)
    eb_bf = b.sb([1, E], BF16)
    bb_bf = b.sb([1, D], BF16)
    tmp_row = b.sb([1, D])
    logits = b.sb([128, E])
    m1 = b.sb([128, 1])
    m2 = b.sb([128, 1])
    s1 = b.sb([128, 1])
    s2 = b.sb([128, 1])
    oh1 = b.sb([128, E])
    oh2 = b.sb([128, E])
    msk = b.sb([128, E])
    wfull = b.sb([128, E])
    zw = b.sb([128, ER], BF16)
    zwT = b.sb([ER, TOK], BF16)

    tr_ps = b.ps([128, 128], BF16)
    sm_ps = b.ps([128, 128])
    mains = [b.ps([128, 512]) for _ in range(4)]
    wbuf = xt[:, 0:2048]
    w_bf = xc[:, 0:1024].bitcast(BF16)
    xres = xc[:, 2048:2560]
    o_sb = xc[:, 3072:3584]

    b.dma(idb[:], identb)
    b.dma(o1f[:], ones1f)
    b.dma(o1b[:], ones1b)
    tmp_eb = b.sb([1, E])
    b.dma(tmp_eb[:], eb)
    b.scopy(eb_bf[:], tmp_eb[:])
    b.dma(tmp_row[:], bb)
    b.scopy(bb_bf[:], tmp_row[:])

    for t in range(NT):
        b.dma(xt[:], xin[t * 128:(t + 1) * 128, :])
        ln_tile(b, xt[:], D, cols, u_bf[:], xc=xc[:], sq=sq[:])
        for k in range(KD):
            b.tr(tr_ps[:], u_bf[:, k * 128:(k + 1) * 128], idb[:])
            b.vcopy(uT[:, k * TOK + t * 128: k * TOK + (t + 1) * 128], tr_ps[:])

    b.dma(sq[:, 0:KD * E].rearrange("p (k e) -> p k e", k=KD),
          gwT.rearrange("(k p) e -> p k e", p=128))
    b.scopy(gw_bf[:], sq[:, 0:KD * E])
    b.dma(sq[:, 0:KD * ER].rearrange("p (k e) -> p k e", k=KD),
          acatT.rearrange("(k p) e -> p k e", p=128))
    b.scopy(acat_bf[:], sq[:, 0:KD * ER])
    b.dma(sq[0:ER, 0:D], bcatT)
    b.scopy(bcat_bf[:], sq[0:ER, 0:D])

    def lhs(k, t):
        return uT[:, k * TOK + t * 128: k * TOK + (t + 1) * 128]

    for t in range(NT):
        for k in range(KD):
            b.mm(sm_ps[:, 0:E], lhs(k, t), gw_bf[:, k * E:(k + 1) * E], k == 0, False)
        b.mm(sm_ps[:, 0:E], o1b[0:1, :], eb_bf[0:1, :], False, True)
        b.vcopy(logits[:], sm_ps[:, 0:E])
        b.red(m1, logits[:], AL.max)
        b.ts(oh1[:], logits[:], m1[:, 0:1], None, AL.is_ge)
        if topk == 1:
            b.vcopy(wfull[:], oh1[:])
        else:
            b.stt(msk[:], oh1[:], -1e30, logits[:], AL.mult, AL.add)
            b.red(m2, msk[:], AL.max)
            b.ts(oh2[:], msk[:], m2[:, 0:1], None, AL.is_ge)
            b.ts(s1, m1[:], m2[:, 0:1], None, AL.subtract)
            b.act(s2, s1, AF.Sigmoid, scale=-1.0)
            b.act(s1, s1, AF.Sigmoid)
            b.ts(wfull[:], oh1[:], s1[:, 0:1], None, AL.mult)
            b.stt(wfull[:], oh2[:], s2[:, 0:1], wfull[:], AL.mult, AL.add)
        if lora_scale != 1.0:
            b.ts(wfull[:], wfull[:], float(lora_scale), None, AL.mult)
        for k in range(KD):
            b.mm(sm_ps[:, 0:ER], lhs(k, t), acat_bf[:, k * ER:(k + 1) * ER], k == 0, k == KD - 1)
        b.tt(zw[:].rearrange("p (e r) -> p e r", e=E),
             sm_ps[:, 0:ER].rearrange("p (e r) -> p e r", e=E),
             wfull[:].unsqueeze(2).broadcast_to([128, E, R]), AL.mult)
        b.tr(tr_ps[0:ER, :], zw[:], idb[:])
        b.vcopy(zwT[:, t * 128:(t + 1) * 128], tr_ps[0:ER, :])

    for tg in range(2):
        tl = list(range(tg * 4, tg * 4 + 4))
        for ob in range(8):
            for kg in range(KD // 4):
                b.dma(wbuf.rearrange("p (k o) -> p k o", k=4),
                      wT[kg * 512:(kg + 1) * 512, ob * 512:(ob + 1) * 512]
                      .rearrange("(k p) o -> p k o", p=128))
                b.scopy(w_bf, wbuf)
                for kk in range(4):
                    k = kg * 4 + kk
                    for i, t in enumerate(tl):
                        b.mm(mains[i][:], lhs(k, t), w_bf[:, kk * 512:(kk + 1) * 512], k == 0, False)
            for i, t in enumerate(tl):
                b.mm(mains[i][:], zwT[:, t * 128:(t + 1) * 128], bcat_bf[:, ob * 512:(ob + 1) * 512], False, False)
                b.mm(mains[i][:], o1b[0:1, :], bb_bf[0:1, ob * 512:(ob + 1) * 512], False, True)
            for i, t in enumerate(tl):
                b.dma(xres, xin[t * 128:(t + 1) * 128, ob * 512:(ob + 1) * 512])
                b.stt(o_sb, mains[i][:], 0.5, xres, AL.mult, AL.add)
                b.dma(xout[t * 128:(t + 1) * 128, ob * 512:(ob + 1) * 512], o_sb)
    return b.fin()


def build_attn0_in():
    DI = 512
    b = B()
    xin = b.inp("xin", [TOK, D])
    winT = b.inp("winT", [D, DI])
    binr = b.inp("binr", [1, DI])
    ln1w = b.inp("ln1w", [1, DI])
    ln1b = b.inp("ln1b", [1, DI])
    identb = b.inp("identb", [128, 128], BF16)
    ones1f = b.inp("ones1f", [1, 128])
    ones1b = b.inp("ones1b", [1, 128], BF16)
    h0 = b.outp("h0", [TOK, DI])
    y0 = b.outp("y0", [TOK, DI])

    uT = b.sb([128, KD * TOK], BF16)
    xt = b.sb([128, D])
    xc = b.sb([128, D])
    sq = b.sb([128, D])
    u_bf = b.sb([128, D], BF16)
    cols = mk_cols(b)
    idb = b.sb([128, 128], BF16)
    o1f = b.sb([1, 128])
    o1b = b.sb([1, 128], BF16)
    bin_bf = b.sb([1, DI], BF16)
    tmp_row = b.sb([1, DI])
    wbuf = b.sb([128, 2048])
    w_bf = b.sb([128, 2048], BF16)
    wbc = b.sb([128, DI])
    bbc = b.sb([128, DI])
    h_sb = [b.sb([128, DI]) for _ in range(NT)]
    y_sb = b.sb([128, DI])
    xc2 = b.sb([128, DI])
    sq2 = b.sb([128, DI])

    tr_ps = b.ps([128, 128], BF16)
    bc_ps = b.ps([128, 512])
    mains = [b.ps([128, 512]) for _ in range(4)]

    b.dma(idb[:], identb)
    b.dma(o1f[:], ones1f)
    b.dma(o1b[:], ones1b)
    b.dma(tmp_row[:], binr)
    b.scopy(bin_bf[:], tmp_row[:])
    b.dma(tmp_row[:], ln1w)
    bcast_row(b, bc_ps, tmp_row, o1f, wbc, DI)
    b.dma(tmp_row[:], ln1b)
    bcast_row(b, bc_ps, tmp_row, o1f, bbc, DI)

    for t in range(NT):
        b.dma(xt[:], xin[t * 128:(t + 1) * 128, :])
        ln_tile(b, xt[:], D, cols, u_bf[:], xc=xc[:], sq=sq[:])
        for k in range(KD):
            b.tr(tr_ps[:], u_bf[:, k * 128:(k + 1) * 128], idb[:])
            b.vcopy(uT[:, k * TOK + t * 128: k * TOK + (t + 1) * 128], tr_ps[:])

    for tg in range(2):
        tl = list(range(tg * 4, tg * 4 + 4))
        for kg in range(KD // 4):
            b.dma(wbuf[:].rearrange("p (k o) -> p k o", k=4),
                  winT[kg * 512:(kg + 1) * 512, :].rearrange("(k p) o -> p k o", p=128))
            b.scopy(w_bf[:], wbuf[:])
            for kk in range(4):
                k = kg * 4 + kk
                for i, t in enumerate(tl):
                    b.mm(mains[i][:], uT[:, k * TOK + t * 128: k * TOK + (t + 1) * 128],
                         w_bf[:, kk * 512:(kk + 1) * 512], k == 0, False)
        for i, t in enumerate(tl):
            b.mm(mains[i][:], o1b[0:1, :], bin_bf[0:1, :], False, True)
            b.vcopy(h_sb[t][:], mains[i][:])
            b.dma(h0[t * 128:(t + 1) * 128, :], h_sb[t][:])
    for t in range(NT):
        ln_tile(b, h_sb[t][:], DI, cols, y_sb[:], wbc=wbc[:], bbc=bbc[:], act=AF.Silu,
                xc=xc2[:], sq=sq2[:])
        b.dma(y0[t * 128:(t + 1) * 128, :], y_sb[:])
    return b.fin()


def build_scan():
    S = 4096
    b = B()
    ycs = b.inp("ycs", [S, 128])
    phr = b.inp("phr", [128, 1])
    phi = b.inp("phi", [128, 1])
    lcr = b.inp("lcr", [128, 1])
    lci = b.inp("lci", [128, 1])
    identf = b.inp("identf", [128, 128])
    convo = b.outp("convo", [S, 128])

    y_sb = b.sb([128, S])
    yT = b.sb([128, S])
    C = b.sb([128, S])
    Sn = b.sb([128, S])
    A = b.sb([128, S])
    Bb = b.sb([128, S])
    RHO = b.sb([128, S])
    tmp1 = b.sb([128, 2048])
    tmp2 = b.sb([128, 2048])
    idf = b.sb([128, 128])
    pr = b.sb([128, 1])
    pi = b.sb([128, 1])
    r2 = b.sb([128, 1])
    rr = b.sb([128, 1])
    th = b.sb([128, 1])
    ir = b.sb([128, 1])
    ere = b.sb([128, 1])
    eim = b.sb([128, 1])
    cn = b.sb([128, 1])
    sn = b.sb([128, 1])
    t1c = b.sb([128, 1])
    t2c = b.sb([128, 1])
    a0 = b.sb([128, 1])
    b0 = b.sb([128, 1])
    tr_ps = b.ps([128, 128])

    b.dma(idf[:], identf)
    b.dma(y_sb[:].rearrange("p (t c) -> p t c", t=KD), ycs.rearrange("(t p) c -> p t c", p=128))
    b.dma(pr[:], phr)
    b.dma(pi[:], phi)
    b.dma(a0[:], lcr)
    b.dma(b0[:], lci)
    for t in range(KD):
        b.tr(tr_ps[:], y_sb[:, t * 128:(t + 1) * 128], idf[:])
        b.vcopy(yT[:, t * 128:(t + 1) * 128], tr_ps[:])

    b.tt(r2[:], pr[:], pr[:], AL.mult)
    b.tt(t1c[:], pi[:], pi[:], AL.mult)
    b.tt(r2[:], r2[:], t1c[:], AL.add)
    b.act(rr, r2, AF.Sqrt)
    b.act(th, rr, AF.Tanh)
    b.recip(ir[:], rr[:])
    b.tt(ere[:], pr[:], ir[:], AL.mult)
    b.tt(eim[:], pi[:], ir[:], AL.mult)

    b.memset(C[:, 0:1], 1.0)
    b.memset(Sn[:, 0:1], 0.0)
    b.vcopy(cn[:], ere[:])
    b.vcopy(sn[:], eim[:])
    n = 1
    while n < S:
        if n > 1:
            b.tt(t1c[:], C[:, n - 1:n], ere[:], AL.mult)
            b.tt(t2c[:], Sn[:, n - 1:n], eim[:], AL.mult)
            b.tt(cn[:], t1c[:], t2c[:], AL.subtract)
            b.tt(t1c[:], C[:, n - 1:n], eim[:], AL.mult)
            b.tt(t2c[:], Sn[:, n - 1:n], ere[:], AL.mult)
            b.tt(sn[:], t1c[:], t2c[:], AL.add)
        m = min(n, S - n)
        b.ts(tmp1[:, 0:m], C[:, 0:m], cn[:, 0:1], None, AL.mult)
        b.ts(tmp2[:, 0:m], Sn[:, 0:m], sn[:, 0:1], None, AL.mult)
        b.tt(C[:, n:n + m], tmp1[:, 0:m], tmp2[:, 0:m], AL.subtract)
        b.ts(tmp1[:, 0:m], Sn[:, 0:m], cn[:, 0:1], None, AL.mult)
        b.ts(tmp2[:, 0:m], C[:, 0:m], sn[:, 0:1], None, AL.mult)
        b.tt(Sn[:, n:n + m], tmp1[:, 0:m], tmp2[:, 0:m], AL.add)
        n *= 2

    b.tt(t1c[:], a0[:], ere[:], AL.mult)
    b.tt(t2c[:], b0[:], eim[:], AL.mult)
    b.tt(r2[:], a0[:], eim[:], AL.mult)
    b.tt(ir[:], b0[:], ere[:], AL.mult)
    b.tt(a0[:], t1c[:], t2c[:], AL.subtract)
    b.tt(b0[:], r2[:], ir[:], AL.add)
    b.ts(b0[:], b0[:], -1.0, None, AL.mult)

    b.memset(RHO[:], 1.0)
    b.ts(RHO[:], RHO[:], th[:, 0:1], None, AL.mult)

    half = 2048
    b.tt(y_sb[:], yT[:], C[:], AL.mult)
    b.scan(A[:, 0:half], RHO[:, 0:half], y_sb[:, 0:half], a0[:, 0:1])
    b.scan(A[:, half:S], RHO[:, half:S], y_sb[:, half:S], A[:, half - 1:half])
    b.tt(y_sb[:], yT[:], Sn[:], AL.mult)
    b.scan(Bb[:, 0:half], RHO[:, 0:half], y_sb[:, 0:half], b0[:, 0:1])
    b.scan(Bb[:, half:S], RHO[:, half:S], y_sb[:, half:S], Bb[:, half - 1:half])

    b.tt(A[:], C[:], A[:], AL.mult)
    b.tt(Bb[:], Sn[:], Bb[:], AL.mult)
    b.tt(A[:], A[:], Bb[:], AL.add)

    for t in range(KD):
        b.tr(tr_ps[:], A[:, t * 128:(t + 1) * 128], idf[:])
        b.vcopy(y_sb[:, t * 128:(t + 1) * 128], tr_ps[:])
    b.dma(convo.rearrange("(t p) c -> p t c", p=128), y_sb[:].rearrange("p (t c) -> p t c", t=KD))
    return b.fin()


def build_mlp():
    DI, DF = 512, 2048
    b = B()
    hin = b.inp("hin", [TOK, DI])
    cin = b.inp("cin", [TOK, DI])
    sclr = b.inp("sclr", [1, DI])
    ln2w = b.inp("ln2w", [1, DI])
    ln2b = b.inp("ln2b", [1, DI])
    w1T = b.inp("w1T", [DI, DF])
    b1r = b.inp("b1r", [1, DF])
    w2T = b.inp("w2T", [DF, DI])
    b2r = b.inp("b2r", [1, DI])
    lnnw = b.inp("lnnw", [1, DI])
    lnnb = b.inp("lnnb", [1, DI])
    identb = b.inp("identb", [128, 128], BF16)
    ones1f = b.inp("ones1f", [1, 128])
    ones1b = b.inp("ones1b", [1, 128], BF16)
    hout = b.outp("hout", [TOK, DI])
    ynext = b.outp("ynext", [TOK, DI])

    idb = b.sb([128, 128], BF16)
    o1f = b.sb([1, 128])
    o1b = b.sb([1, 128], BF16)
    cols = mk_cols(b)
    h_t = b.sb([128, DI])
    c_t = b.sb([128, DI])
    xc = b.sb([128, DF])
    sq = b.sb([128, DF])
    sclbc = b.sb([128, DI])
    w2bc = b.sb([128, DI])
    b2bc = b.sb([128, DI])
    wnbc = b.sb([128, DI])
    bnbc = b.sb([128, DI])
    tmp_row = b.sb([1, DF])
    z_bf = b.sb([128, DI], BF16)
    zT = b.sb([128, DI], BF16)
    w1_sb = b.sb([128, 4 * DF])
    w1_bf = b.sb([128, 4 * DF], BF16)
    w2_sb = b.sb([128, 16 * DI])
    w2_bf = b.sb([128, 16 * DI], BF16)
    b1_bf = b.sb([1, DF], BF16)
    b2_bf = b.sb([1, DI], BF16)
    f_bf = b.sb([128, DF], BF16)
    fT = b.sb([128, DF], BF16)
    y_sb = b.sb([128, DI])

    tr_ps = b.ps([128, 128], BF16)
    bc_ps = b.ps([128, 512])
    mainA = b.ps([128, 512])
    mainB = b.ps([128, 512])

    b.dma(idb[:], identb)
    b.dma(o1f[:], ones1f)
    b.dma(o1b[:], ones1b)
    for (row, dst) in [(sclr, sclbc), (ln2w, w2bc), (ln2b, b2bc), (lnnw, wnbc), (lnnb, bnbc)]:
        b.dma(tmp_row[0:1, 0:DI], row)
        bcast_row(b, bc_ps, tmp_row[0:1, 0:DI], o1f, dst, DI)
    b.dma(w1_sb[:].rearrange("p (k f) -> p k f", k=4), w1T.rearrange("(k p) f -> p k f", p=128))
    b.scopy(w1_bf[:], w1_sb[:])
    b.dma(w2_sb[:].rearrange("p (k f) -> p k f", k=16), w2T.rearrange("(k p) f -> p k f", p=128))
    b.scopy(w2_bf[:], w2_sb[:])
    b.dma(tmp_row[:], b1r)
    b.scopy(b1_bf[:], tmp_row[:])
    b.dma(tmp_row[0:1, 0:DI], b2r)
    b.scopy(b2_bf[:], tmp_row[0:1, 0:DI])

    for t in range(NT):
        b.dma(h_t[:], hin[t * 128:(t + 1) * 128, :])
        b.dma(c_t[:], cin[t * 128:(t + 1) * 128, :])
        b.tt(c_t[:], c_t[:], sclbc[:], AL.mult)
        b.tt(h_t[:], h_t[:], c_t[:], AL.add)
        ln_tile(b, h_t[:], DI, cols, z_bf[:], wbc=w2bc[:], bbc=b2bc[:],
                xc=xc[:, 0:DI], sq=sq[:, 0:DI])
        for k in range(4):
            b.tr(tr_ps[:], z_bf[:, k * 128:(k + 1) * 128], idb[:])
            b.vcopy(zT[:, k * 128:(k + 1) * 128], tr_ps[:])
        for o in range(4):
            for k in range(4):
                b.mm(mainA[:], zT[:, k * 128:(k + 1) * 128],
                     w1_bf[:, k * DF + o * 512: k * DF + (o + 1) * 512], k == 0, False)
            b.mm(mainA[:], o1b[0:1, :], b1_bf[0:1, o * 512:(o + 1) * 512], False, True)
            b.act(f_bf[:, o * 512:(o + 1) * 512], mainA[:], AF.Silu)
        for k in range(16):
            b.tr(tr_ps[:], f_bf[:, k * 128:(k + 1) * 128], idb[:])
            b.vcopy(fT[:, k * 128:(k + 1) * 128], tr_ps[:])
        for k in range(16):
            b.mm(mainB[:], fT[:, k * 128:(k + 1) * 128],
                 w2_bf[:, k * DI:(k + 1) * DI], k == 0, False)
        b.mm(mainB[:], o1b[0:1, :], b2_bf[0:1, :], False, True)
        b.tt(h_t[:], h_t[:], mainB[:], AL.add)
        b.dma(hout[t * 128:(t + 1) * 128, :], h_t[:])
        ln_tile(b, h_t[:], DI, cols, y_sb[:], wbc=wnbc[:], bbc=bnbc[:], act=AF.Silu,
                xc=xc[:, 0:DI], sq=sq[:, 0:DI])
        b.dma(ynext[t * 128:(t + 1) * 128, :], y_sb[:])
    return b.fin()


def build_attn0_final():
    DI = 512
    b = B()
    hin = b.inp("hin", [TOK, DI])
    x1 = b.inp("x1", [TOK, D])
    lnlw = b.inp("lnlw", [1, DI])
    lnlb = b.inp("lnlb", [1, DI])
    woutT = b.inp("woutT", [DI, D])
    boutr = b.inp("boutr", [1, D])
    identb = b.inp("identb", [128, 128], BF16)
    ones1f = b.inp("ones1f", [1, 128])
    ones1b = b.inp("ones1b", [1, 128], BF16)
    x2 = b.outp("x2", [TOK, D])

    idb = b.sb([128, 128], BF16)
    o1f = b.sb([1, 128])
    o1b = b.sb([1, 128], BF16)
    cols = mk_cols(b)
    h_t = b.sb([128, DI])
    xc = b.sb([128, DI])
    sq = b.sb([128, DI])
    wbc = b.sb([128, DI])
    bbc = b.sb([128, DI])
    tmp_row = b.sb([1, D])
    hf_bf = b.sb([128, DI], BF16)
    hT = b.sb([128, DI], BF16)
    bout_bf = b.sb([1, D], BF16)
    xres = b.sb([128, 512])
    o_sb = b.sb([128, 512])

    tr_ps = b.ps([128, 128], BF16)
    bc_ps = b.ps([128, 512])
    main = b.ps([128, 512])

    b.dma(idb[:], identb)
    b.dma(o1f[:], ones1f)
    b.dma(o1b[:], ones1b)
    b.dma(tmp_row[0:1, 0:DI], lnlw)
    bcast_row(b, bc_ps, tmp_row[0:1, 0:DI], o1f, wbc, DI)
    b.dma(tmp_row[0:1, 0:DI], lnlb)
    bcast_row(b, bc_ps, tmp_row[0:1, 0:DI], o1f, bbc, DI)
    b.dma(tmp_row[:], boutr)
    b.scopy(bout_bf[:], tmp_row[:])
    w_f32 = b.sb([128, D])
    wout_bf = b.sb([128, 4 * D], BF16)
    for k in range(4):
        b.dma(w_f32[:], woutT[k * 128:(k + 1) * 128, :])
        b.scopy(wout_bf[:, k * D:(k + 1) * D], w_f32[:])

    for t in range(NT):
        b.dma(h_t[:], hin[t * 128:(t + 1) * 128, :])
        ln_tile(b, h_t[:], DI, cols, hf_bf[:], wbc=wbc[:], bbc=bbc[:], xc=xc[:], sq=sq[:])
        for k in range(4):
            b.tr(tr_ps[:], hf_bf[:, k * 128:(k + 1) * 128], idb[:])
            b.vcopy(hT[:, k * 128:(k + 1) * 128], tr_ps[:])
        for ob in range(8):
            for k in range(4):
                b.mm(main[:], hT[:, k * 128:(k + 1) * 128],
                     wout_bf[:, k * D + ob * 512: k * D + (ob + 1) * 512], k == 0, False)
            b.mm(main[:], o1b[0:1, :], bout_bf[0:1, ob * 512:(ob + 1) * 512], False, True)
            b.dma(xres[:], x1[t * 128:(t + 1) * 128, ob * 512:(ob + 1) * 512])
            b.tt(o_sb[:], main[:], xres[:], AL.add)
            b.dma(x2[t * 128:(t + 1) * 128, ob * 512:(ob + 1) * 512], o_sb[:])
    return b.fin()


def build_rwkv_proj():
    b = B()
    xin = b.inp("xin", [TOK, D])
    halo = b.inp("halo", [1, D])
    wts = [b.inp(nm, [D, D]) for nm in ["wkT", "wvT", "wrT", "wgT"]]
    tms = [b.inp(nm, [128, KD]) for nm in ["tmk", "tmv", "tmr", "tmg"]]
    identf = b.inp("identf", [128, 128])
    outs = [b.outp(nm, [TOK, D]) for nm in ["ko", "vo", "ro", "go"]]
    hscr = b.scr("hscr", [1, D])

    idf = b.sb([128, 128])
    cols = mk_cols(b)
    xt = b.sb([128, D])
    xc = b.sb([128, D])
    uTall = b.sb([128, KD * (TOK + 1)], BF16)
    mixT = b.sb([128, KD * TOK], BF16)
    tm_sb = [b.sb([128, KD]) for _ in range(4)]
    om_sb = [b.sb([128, KD]) for _ in range(4)]
    halocols = b.sb([128, KD])
    hcols = mk_cols(b, 1)
    mtmp = b.sb([128, 128])

    mains = [b.ps([128, 512]) for _ in range(8)]
    wbuf = xt[:, 0:2048]
    w_bf = xc[:, 0:1024].bitcast(BF16)
    o_sb = xc[:, 2048:2560]

    b.dma(idf[:], identf)
    for m in range(4):
        b.dma(tm_sb[m][:], tms[m])
        b.ts(om_sb[m][:], tm_sb[m][:], -1.0, 1.0, AL.mult, AL.add)

    b.dma(xt[0:1, :], halo)
    ln_tile(b, xt[0:1, :], D, hcols, xc[0:1, :], xc=xc[0:1, :], sq=xt[0:1, :])
    b.dma(hscr, xc[0:1, :])
    b.dma(halocols[:], hscr.rearrange("o (f p) -> (o p) f", p=128),
          allow_slow_non_contiguous=True)
    for k in range(KD):
        b.scopy(uTall[:, k * (TOK + 1):k * (TOK + 1) + 1], halocols[:, k:k + 1])

    base = lambda k: k * (TOK + 1)
    for t in range(NT):
        b.dma(xt[:], xin[t * 128:(t + 1) * 128, :])
        ln_tile(b, xt[:], D, cols, xc[:], xc=xc[:], sq=xt[:])
        for k in range(KD):
            b.tr(mains[0][:, 0:128], xc[:, k * 128:(k + 1) * 128], idf[:])
            b.scopy(uTall[:, base(k) + 1 + t * 128: base(k) + 1 + (t + 1) * 128],
                    mains[0][:, 0:128])

    for m in range(4):
        for k in range(KD):
            for t in range(NT):
                cur = uTall[:, base(k) + 1 + t * 128: base(k) + 1 + (t + 1) * 128]
                sh = uTall[:, base(k) + t * 128: base(k) + (t + 1) * 128]
                b.ts(mtmp[:], sh, om_sb[m][:, k:k + 1], None, AL.mult)
                b.stt(mixT[:, k * TOK + t * 128: k * TOK + (t + 1) * 128],
                      cur, tm_sb[m][:, k:k + 1], mtmp[:], AL.mult, AL.add)
        for ob in range(8):
            for kg in range(KD // 4):
                b.dma(wbuf.rearrange("p (k o) -> p k o", k=4),
                      wts[m][kg * 512:(kg + 1) * 512, ob * 512:(ob + 1) * 512]
                      .rearrange("(k p) o -> p k o", p=128))
                b.scopy(w_bf, wbuf)
                for kk in range(4):
                    k = kg * 4 + kk
                    for t in range(NT):
                        b.mm(mains[t][:], mixT[:, k * TOK + t * 128: k * TOK + (t + 1) * 128],
                             w_bf[:, kk * 512:(kk + 1) * 512], k == 0, k == KD - 1)
            for t in range(NT):
                if m == 3:
                    b.act(o_sb, mains[t][:], AF.Silu)
                else:
                    b.vcopy(o_sb, mains[t][:])
                b.dma(outs[m][t * 128:(t + 1) * 128, ob * 512:(ob + 1) * 512], o_sb)
    return b.fin()


def build_wkv():
    S, HD, NH, C = 4096, 64, 8, 128
    NJ = S // C
    b = B()
    kk = b.inp("kk", [2, S, NH * HD])
    vv = b.inp("vv", [2, S, NH * HD])
    rr = b.inp("rr", [2, S, NH * HD])
    td8 = b.inp("td8", [1, NH * HD])
    tf8 = b.inp("tf8", [1, NH * HD])
    wexp = b.inp("wexp", [1, 6 * 128])
    offpad = b.inp("offpad", [1, 784])
    mkeep = b.inp("mkeep", [128, 128])
    m16off = b.inp("m16off", [128, 128])
    mdiag = b.inp("mdiag", [128, 128])
    identf = b.inp("identf", [128, 128])
    ones1f = b.inp("ones1f", [1, 128])
    wo = b.outp("wo", [2, S, NH * HD])

    idf = b.sb([128, 128])
    o1f = b.sb([1, 128])
    exp_sb = b.sb([1, 6 * 128])
    off_sb = b.sb([1, 784])
    mk_sb = b.sb([128, 128])
    m16_sb = b.sb([128, 128])
    md_sb = b.sb([128, 128])
    td_sb = b.sb([1, NH * HD])
    tf_sb = b.sb([1, NH * HD])
    lnw = b.sb([1, NH * HD])

    DPOWi = [b.sb([HD, 128]) for _ in range(NH)]
    DPOW8 = [b.sb([HD, 128]) for _ in range(NH)]
    DPOW8K = [b.sb([HD, 128]) for _ in range(NH)]
    DPOWm8 = [b.sb([HD, 128]) for _ in range(NH)]
    DPOWm16 = [b.sb([HD, 128]) for _ in range(NH)]
    DTF = [b.sb([HD, 128]) for _ in range(NH)]
    OFF16 = [b.sb([HD, 784]) for _ in range(NH)]
    SPOWrev = [b.sb([128, HD]) for _ in range(NH)]
    w128c = [b.sb([HD, 1]) for _ in range(NH)]

    r_sb = b.sb([128, HD])
    k_sb = b.sb([128, HD])
    v_sb = b.sb([128, HD])
    vbf = b.sb([128, HD], BF16)
    kvbf = b.sb([128, HD], BF16)
    kT_sb = b.sb([HD, 128])
    Rtr = b.sb([HD, 128], BF16)
    Rm8 = b.sb([HD, 128], BF16)
    Rm16 = b.sb([HD, 128], BF16)
    Rtf = b.sb([HD, 128], BF16)
    Rint = b.sb([HD, 128], BF16)
    Ktr = b.sb([HD, 128], BF16)
    Kpl = b.sb([HD, 128], BF16)
    KOFF = b.sb([HD, 784], BF16)
    A_bf = b.sb([128, 128], BF16)
    tmp_bf = b.sb([128, 128], BF16)
    S_sb = b.sb([HD, HD])
    S_bf = b.sb([HD, HD], BF16)
    o_sb = b.sb([128, HD])

    At = b.ps([128, 128])
    o_ps = b.ps([128, HD])
    B_ps = b.ps([HD, HD])
    trR = b.ps([HD, 128])
    trK = b.ps([HD, 128])

    b.dma(idf[:], identf)
    b.dma(o1f[:], ones1f)
    b.dma(exp_sb[:], wexp)
    b.dma(off_sb[:], offpad)
    b.dma(mk_sb[:], mkeep)
    b.dma(m16_sb[:], m16off)
    b.dma(md_sb[:], mdiag)
    b.dma(td_sb[:], td8)
    b.dma(tf_sb[:], tf8)
    b.act(lnw[:], td_sb[:], AF.Exp)
    b.ts(lnw[:], lnw[:], -1.0, None, AL.mult)

    for h in range(NH):
        lr = lnw[0:1, h * HD:(h + 1) * HD]
        for (dst, row) in [(DPOWi[h], exp_sb[0:1, 0:128]), (DPOW8[h], exp_sb[0:1, 256:384]),
                           (DPOW8K[h], exp_sb[0:1, 384:512]), (DPOWm8[h], exp_sb[0:1, 512:640]),
                           (DPOWm16[h], exp_sb[0:1, 640:768])]:
            b.mm(At[0:HD, 0:128], lr, row, True, True)
            b.act(dst[:], At[0:HD, 0:128], AF.Exp)
        for j in range(7):
            b.mm(At[0:HD, 0:112], lr, off_sb[0:1, j * 112:(j + 1) * 112], True, True)
            b.act(OFF16[h][:, j * 112:(j + 1) * 112], At[0:HD, 0:112], AF.Exp)
        b.mm(At[0:HD, 0:128], tf_sb[0:1, h * HD:(h + 1) * HD], o1f[0:1, :], True, True)
        b.vcopy(DTF[h][:], At[0:HD, 0:128])
        b.mm(At[0:128, 0:HD], exp_sb[0:1, 128:256], lr, True, True)
        b.act(SPOWrev[h][:], At[0:128, 0:HD], AF.Exp)
        b.mm(At[0:HD, 0:1], lr, o1f[0:1, 0:1], True, True)
        b.act(w128c[h][:], At[0:HD, 0:1], AF.Exp, scale=128.0)

    for bb in range(2):
        for h in range(NH):
            b.memset(S_sb[:], 0.0)
            for j in range(NJ):
                rows = slice(j * 128, (j + 1) * 128)
                colsl = slice(h * HD, (h + 1) * HD)
                b.dma(r_sb[:], rr[bb, rows, colsl])
                b.dma(k_sb[:], kk[bb, rows, colsl])
                b.dma(v_sb[:], vv[bb, rows, colsl])
                b.scopy(vbf[:], v_sb[:])
                b.tt(kvbf[:], k_sb[:], SPOWrev[h][:], AL.mult)
                b.tr(trR[:], r_sb[:], idf[:])
                b.tr(trK[:], k_sb[:], idf[:])
                b.tt(Rtr[:], trR[:], DPOW8[h][:], AL.mult)
                b.tt(Rm8[:], trR[:], DPOWm8[h][:], AL.mult)
                b.tt(Rm16[:], trR[:], DPOWm16[h][:], AL.mult)
                b.tt(Rtf[:], trR[:], DTF[h][:], AL.mult)
                b.tt(Rint[:], trR[:], DPOWi[h][:], AL.mult)
                b.tt(Ktr[:], trK[:], DPOW8K[h][:], AL.mult)
                b.scopy(Kpl[:], trK[:])
                b.scopy(kT_sb[:], trK[:])
                b.tt(KOFF[:].rearrange("d (j t) -> d j t", j=7),
                     kT_sb[:, 0:112].unsqueeze(1).broadcast_to([HD, 7, 112]),
                     OFF16[h][:].rearrange("d (j t) -> d j t", j=7), AL.mult)
                b.mm(At[:], Ktr[:], Rtr[:], True, True)
                for J in range(1, 8):
                    b.mm(At[0:16 * J, 16 * J:16 * J + 16],
                         KOFF[:, (J - 1) * 112:(J - 1) * 112 + 16 * J],
                         Rm16[:, 16 * J:16 * J + 16], True, True)
                b.tt(A_bf[:], At[:], mk_sb[:], AL.mult)
                b.mm(At[:], Ktr[:], Rm8[:], True, True)
                b.tt(tmp_bf[:], At[:], m16_sb[:], AL.mult)
                b.tt(A_bf[:], A_bf[:], tmp_bf[:], AL.add)
                b.mm(At[:], Kpl[:], Rtf[:], True, True)
                b.tt(tmp_bf[:], At[:], md_sb[:], AL.mult)
                b.tt(A_bf[:], A_bf[:], tmp_bf[:], AL.add)
                b.mm(o_ps[:], A_bf[:], vbf[:], True, j == 0)
                if j > 0:
                    b.mm(o_ps[:], Rint[:], S_bf[:], False, True)
                b.scopy(o_sb[:], o_ps[:])
                b.dma(wo[bb, rows, colsl], o_sb[:])
                b.mm(B_ps[:], kvbf[:], vbf[:], True, True)
                b.stt(S_sb[:], S_sb[:], w128c[h][:, 0:1], B_ps[:], AL.mult, AL.add)
                b.scopy(S_bf[:], S_sb[:])
    return b.fin()


def build_rwkv_out():
    b = B()
    wkvi = b.inp("wkvi", [TOK, D])
    gi = b.inp("gi", [TOK, D])
    x4 = b.inp("x4", [TOK, D])
    gnw = b.inp("gnw", [1, D])
    gnb = b.inp("gnb", [1, D])
    woT = b.inp("woT", [D, D])
    identb = b.inp("identb", [128, 128], BF16)
    ones1f = b.inp("ones1f", [1, 128])
    x5 = b.outp("x5", [TOK, D])

    idb = b.sb([128, 128], BF16)
    o1f = b.sb([1, 128])
    wk_t = b.sb([128, D])
    g_t = b.sb([128, D])
    xc = b.sb([128, D])
    sq = b.sb([128, D])
    mu_g = b.sb([128, 64])
    var_g = b.sb([128, 64])
    gnwbc = b.sb([128, D])
    gnbbc = b.sb([128, D])
    tmp_row = b.sb([1, D])
    y_bf = b.sb([128, D], BF16)
    yT = b.sb([128, KD * TOK], BF16)
    wbuf = b.sb([128, 2048])
    w_bf = b.sb([128, 2048], BF16)
    xres = b.sb([128, 512])
    o_sb = b.sb([128, 512])

    tr_ps = b.ps([128, 128], BF16)
    bc_ps = b.ps([128, 512])
    mains = [b.ps([128, 512]) for _ in range(4)]

    b.dma(idb[:], identb)
    b.dma(o1f[:], ones1f)
    b.dma(tmp_row[:], gnw)
    bcast_row(b, bc_ps, tmp_row, o1f, gnwbc, D)
    b.dma(tmp_row[:], gnb)
    bcast_row(b, bc_ps, tmp_row, o1f, gnbbc, D)

    r3 = lambda ap: ap.rearrange("p (g i) -> p g i", g=64)
    for t in range(NT):
        b.dma(wk_t[:], wkvi[t * 128:(t + 1) * 128, :])
        b.dma(g_t[:], gi[t * 128:(t + 1) * 128, :])
        b.red(mu_g[:], r3(wk_t[:]), AL.add)
        b.ts(mu_g[:], mu_g[:], 1.0 / 64, None, AL.mult)
        b.tt(r3(xc[:]), r3(wk_t[:]), mu_g[:].unsqueeze(2).broadcast_to([128, 64, 64]), AL.subtract)
        b.act(sq[:], xc[:], AF.Square)
        b.red(var_g[:], r3(sq[:]), AL.add)
        b.ts(var_g[:], var_g[:], 1.0 / 64, 1e-5, AL.mult, AL.add)
        b.act(var_g[:], var_g[:], AF.Ln)
        b.act(var_g[:], var_g[:], AF.Exp, scale=-0.5)
        b.tt(r3(xc[:]), r3(xc[:]), var_g[:].unsqueeze(2).broadcast_to([128, 64, 64]), AL.mult)
        b.tt(xc[:], xc[:], gnwbc[:], AL.mult)
        b.tt(xc[:], xc[:], gnbbc[:], AL.add)
        b.tt(y_bf[:], xc[:], g_t[:], AL.mult)
        for k in range(KD):
            b.tr(tr_ps[:], y_bf[:, k * 128:(k + 1) * 128], idb[:])
            b.vcopy(yT[:, k * TOK + t * 128: k * TOK + (t + 1) * 128], tr_ps[:])

    for tg in range(2):
        tl = list(range(tg * 4, tg * 4 + 4))
        for ob in range(8):
            for kg in range(KD // 4):
                b.dma(wbuf[:].rearrange("p (k o) -> p k o", k=4),
                      woT[kg * 512:(kg + 1) * 512, ob * 512:(ob + 1) * 512]
                      .rearrange("(k p) o -> p k o", p=128))
                b.scopy(w_bf[:], wbuf[:])
                for kk in range(4):
                    k = kg * 4 + kk
                    for i, t in enumerate(tl):
                        b.mm(mains[i][:], yT[:, k * TOK + t * 128: k * TOK + (t + 1) * 128],
                             w_bf[:, kk * 512:(kk + 1) * 512], k == 0, k == KD - 1)
            for i, t in enumerate(tl):
                b.dma(xres[:], x4[t * 128:(t + 1) * 128, ob * 512:(ob + 1) * 512])
                b.tt(o_sb[:], mains[i][:], xres[:], AL.add)
                b.dma(x5[t * 128:(t + 1) * 128, ob * 512:(ob + 1) * 512], o_sb[:])
    return b.fin()


_cache = {}


def _get(name, fn):
    if name not in _cache:
        _cache[name] = fn()
    return _cache[name]


def _run(nc, in_maps):
    res = run_bass_kernel_spmd(nc, in_maps, core_ids=list(range(NCORE)))
    return res.results


def _consts():
    c = {}
    c['identb'] = np.eye(128).astype(BF)
    c['identf'] = np.eye(128, dtype=np.float32)
    c['ones1f'] = np.ones((1, 128), dtype=np.float32)
    c['ones1b'] = np.ones((1, 128)).astype(BF)
    return c


def _shards(x):
    return [np.ascontiguousarray(x[c * TOK:(c + 1) * TOK]) for c in range(NCORE)]


def _gather(res, key):
    return np.concatenate([np.asarray(r[key], dtype=np.float32) for r in res], axis=0)


def launch_ffn(x, wT, gwT, acatT, bcatT, eb, bb, cfg, consts):
    nc = _get(f"ffn{cfg}", lambda: build_ffn(*cfg))
    com = dict(wT=wT, gwT=gwT, acatT=acatT, bcatT=bcatT, eb=eb, bb=bb,
               identb=consts['identb'], ones1f=consts['ones1f'], ones1b=consts['ones1b'])
    maps = [dict(xin=s, **com) for s in _shards(x)]
    return _gather(_run(nc, maps), "xout")


def kernel(_debug_cb=None, **inp):
    _cb = _debug_cb if _debug_cb is not None else (lambda *a: None)
    consts = _consts()
    f32 = np.float32
    x = np.ascontiguousarray(np.asarray(inp["L0_attn_x"], f32).reshape(2 * 4096, D))

    l0_wT = np.ascontiguousarray(np.asarray(inp["L0_ffn_W_up"], f32).T)
    l0_gwT = np.ascontiguousarray(np.asarray(inp["L0_ffn_gate_w"], f32).T)
    l0_eb = np.ascontiguousarray(np.asarray(inp["L0_ffn_expert_bias"], f32).reshape(1, 4))
    l0_acatT = np.ascontiguousarray(np.asarray(inp["L0_ffn_lora_A"], f32).reshape(32, D).T)
    l0_bcatT = np.ascontiguousarray(
        np.asarray(inp["L0_ffn_lora_B"], f32).transpose(0, 2, 1).reshape(32, D))
    l0_bb = np.zeros((1, D), f32)
    cfg0 = (4, 8, 1, 1.0, True)

    l1_wT = np.ascontiguousarray(np.asarray(inp["L1_ffn_base_w"], f32).T)
    l1_gwT = np.ascontiguousarray(np.asarray(inp["L1_ffn_gate_w"], f32).T)
    l1_eb = np.zeros((1, 8), f32)
    l1_acatT = np.ascontiguousarray(np.asarray(inp["L1_ffn_lora_A"], f32).reshape(128, D).T)
    l1_bcatT = np.ascontiguousarray(
        np.asarray(inp["L1_ffn_lora_B"], f32).transpose(0, 2, 1).reshape(128, D))
    l1_bb = np.ascontiguousarray(np.asarray(inp["L1_ffn_base_b"], f32).reshape(1, D))
    cfg1 = (8, 16, 2, 2.0, False)

    x1 = launch_ffn(x, l0_wT, l0_gwT, l0_acatT, l0_bcatT, l0_eb, l0_bb, cfg0, consts)

    _cb("x1", x1)
    nc2 = _get("a0in", build_attn0_in)
    winT = np.ascontiguousarray(np.asarray(inp["L0_attn_Win"], f32).T)
    binr = np.ascontiguousarray(np.asarray(inp["L0_attn_bin"], f32).reshape(1, 512))
    ln1w = np.asarray(inp["L0_attn_ln1_w"], f32)
    ln1b = np.asarray(inp["L0_attn_ln1_b"], f32)
    com = dict(winT=winT, binr=binr,
               ln1w=np.ascontiguousarray(ln1w[0].reshape(1, 512)),
               ln1b=np.ascontiguousarray(ln1b[0].reshape(1, 512)),
               identb=consts['identb'], ones1f=consts['ones1f'], ones1b=consts['ones1b'])
    res = _run(nc2, [dict(xin=s, **com) for s in _shards(x1)])
    h = _gather(res, "h0")
    y = _gather(res, "y0")
    _cb("h0", h)
    _cb("y0", y)

    nc_scan = _get("scan", build_scan)
    nc_mlp = _get("mlp", build_mlp)
    phr = np.asarray(inp["L0_attn_phz_re"], f32)
    phi = np.asarray(inp["L0_attn_phz_im"], f32)
    lcr = np.asarray(inp["L0_attn_lc_re"], f32)
    lci = np.asarray(inp["L0_attn_lc_im"], f32)
    scl = np.asarray(inp["L0_attn_sc_lin"], f32)
    ln2w = np.asarray(inp["L0_attn_ln2_w"], f32)
    ln2b = np.asarray(inp["L0_attn_ln2_b"], f32)
    W1 = np.asarray(inp["L0_attn_W1"], f32)
    b1 = np.asarray(inp["L0_attn_b1"], f32)
    W2 = np.asarray(inp["L0_attn_W2"], f32)
    b2 = np.asarray(inp["L0_attn_b2"], f32)
    for t in range(4):
        maps = []
        for c in range(NCORE):
            bb_, cg = c // 4, c % 4
            sl = slice(cg * 128, (cg + 1) * 128)
            maps.append(dict(
                ycs=np.ascontiguousarray(y[bb_ * 4096:(bb_ + 1) * 4096, sl]),
                phr=np.ascontiguousarray(phr[t, sl].reshape(128, 1)),
                phi=np.ascontiguousarray(phi[t, sl].reshape(128, 1)),
                lcr=np.ascontiguousarray(lcr[t, sl].reshape(128, 1)),
                lci=np.ascontiguousarray(lci[t, sl].reshape(128, 1)),
                identf=consts['identf']))
        res = _run(nc_scan, maps)
        conv = np.zeros((8192, 512), f32)
        for c in range(NCORE):
            bb_, cg = c // 4, c % 4
            conv[bb_ * 4096:(bb_ + 1) * 4096, cg * 128:(cg + 1) * 128] = \
                np.asarray(res[c]["convo"], f32)
        tn = min(t + 1, 3)
        com = dict(sclr=np.ascontiguousarray(scl[t].reshape(1, 512)),
                   ln2w=np.ascontiguousarray(ln2w[t].reshape(1, 512)),
                   ln2b=np.ascontiguousarray(ln2b[t].reshape(1, 512)),
                   w1T=np.ascontiguousarray(W1[t].T), b1r=np.ascontiguousarray(b1[t].reshape(1, 2048)),
                   w2T=np.ascontiguousarray(W2[t].T), b2r=np.ascontiguousarray(b2[t].reshape(1, 512)),
                   lnnw=np.ascontiguousarray(ln1w[tn].reshape(1, 512)),
                   lnnb=np.ascontiguousarray(ln1b[tn].reshape(1, 512)),
                   identb=consts['identb'], ones1f=consts['ones1f'], ones1b=consts['ones1b'])
        hs = _shards(h)
        cs = _shards(conv)
        _cb(f"conv{t}", conv)
        res = _run(nc_mlp, [dict(hin=hs[c], cin=cs[c], **com) for c in range(NCORE)])
        h = _gather(res, "hout")
        y = _gather(res, "ynext")
        _cb(f"h{t + 1}", h)
        _cb(f"y{t + 1}", y)

    nc_f = _get("a0fin", build_attn0_final)
    com = dict(lnlw=np.ascontiguousarray(np.asarray(inp["L0_attn_lnl_w"], f32).reshape(1, 512)),
               lnlb=np.ascontiguousarray(np.asarray(inp["L0_attn_lnl_b"], f32).reshape(1, 512)),
               woutT=np.ascontiguousarray(np.asarray(inp["L0_attn_Wout"], f32).T),
               boutr=np.ascontiguousarray(np.asarray(inp["L0_attn_bout"], f32).reshape(1, D)),
               identb=consts['identb'], ones1f=consts['ones1f'], ones1b=consts['ones1b'])
    hs = _shards(h)
    x1s = _shards(x1)
    res = _run(nc_f, [dict(hin=hs[c], x1=x1s[c], **com) for c in range(NCORE)])
    x2 = _gather(res, "x2")
    _cb("x2", x2)

    x3 = launch_ffn(x2, l0_wT, l0_gwT, l0_acatT, l0_bcatT, l0_eb, l0_bb, cfg0, consts)
    _cb("x3", x3)
    x4 = launch_ffn(x3, l1_wT, l1_gwT, l1_acatT, l1_bcatT, l1_eb, l1_bb, cfg1, consts)
    _cb("x4", x4)

    nc_p = _get("rwkvp", build_rwkv_proj)
    w_tm = {}
    for nm, key in [("tmk", "L1_attn_time_mix_key"), ("tmv", "L1_attn_time_mix_value"),
                    ("tmr", "L1_attn_time_mix_receptance"), ("tmg", "L1_attn_time_mix_gate")]:
        w_tm[nm] = np.ascontiguousarray(np.asarray(inp[key], f32).reshape(KD, 128).T)
    com = dict(wkT=np.ascontiguousarray(np.asarray(inp["L1_attn_w_key"], f32).T),
               wvT=np.ascontiguousarray(np.asarray(inp["L1_attn_w_value"], f32).T),
               wrT=np.ascontiguousarray(np.asarray(inp["L1_attn_w_receptance"], f32).T),
               wgT=np.ascontiguousarray(np.asarray(inp["L1_attn_w_gate"], f32).T),
               identf=consts['identf'], **w_tm)
    x4s = _shards(x4)
    maps = []
    for c in range(NCORE):
        if c % 4 == 0:
            halo = np.zeros((1, D), f32)
        else:
            halo = np.ascontiguousarray(x4[c * TOK - 1:c * TOK, :])
        maps.append(dict(xin=x4s[c], halo=halo, **com))
    res = _run(nc_p, maps)
    kf = _gather(res, "ko")
    vf = _gather(res, "vo")
    rf = _gather(res, "ro")
    gf = _gather(res, "go")
    _cb("kf", kf)
    _cb("vf", vf)
    _cb("rf", rf)
    _cb("gf", gf)

    nc_w = _get("wkv", build_wkv)
    td = np.asarray(inp["L1_attn_time_decay"], f32)
    tf = np.asarray(inp["L1_attn_time_faaaa"], f32)
    t_ar = np.arange(128, dtype=f32)
    wexp = np.stack([t_ar, 127 - t_ar, (t_ar % 8) - 8, 7 - (t_ar % 8),
                     t_ar % 8, t_ar % 16]).astype(f32).reshape(1, 768)
    offpad = np.full((1, 784), 300.0, f32)
    for J in range(1, 8):
        tt_ = np.arange(16 * J, dtype=f32)
        offpad[0, (J - 1) * 112:(J - 1) * 112 + 16 * J] = 16 * J - 1 - tt_
    ti = np.arange(128)
    tgrid, igrid = np.meshgrid(ti, ti, indexing='ij')
    mkeep = ((tgrid < igrid) & ~((tgrid // 16 == igrid // 16) & (tgrid % 16 < 8)
                                 & (igrid % 16 >= 8))).astype(f32)
    m16off = ((tgrid // 16 == igrid // 16) & (tgrid % 16 < 8) & (igrid % 16 >= 8)).astype(f32)
    mdiag = np.eye(128, dtype=f32)
    k3 = kf.reshape(2, 4096, D)
    v3 = vf.reshape(2, 4096, D)
    r3_ = rf.reshape(2, 4096, D)
    maps = []
    for c in range(NCORE):
        sl = slice(c * 512, (c + 1) * 512)
        hsl = slice(c * 8, (c + 1) * 8)
        maps.append(dict(kk=np.ascontiguousarray(k3[:, :, sl]),
                         vv=np.ascontiguousarray(v3[:, :, sl]),
                         rr=np.ascontiguousarray(r3_[:, :, sl]),
                         td8=np.ascontiguousarray(td[hsl].reshape(1, 512)),
                         tf8=np.ascontiguousarray(tf[hsl].reshape(1, 512)),
                         wexp=wexp, offpad=offpad, mkeep=mkeep, m16off=m16off, mdiag=mdiag,
                         identf=consts['identf'], ones1f=consts['ones1f']))
    res = _run(nc_w, maps)
    wkv = np.zeros((2, 4096, D), f32)
    for c in range(NCORE):
        wkv[:, :, c * 512:(c + 1) * 512] = np.asarray(res[c]["wo"], f32)
    wkv = wkv.reshape(8192, D)
    _cb("wkv", wkv)

    nc_o = _get("rwkvo", build_rwkv_out)
    com = dict(gnw=np.ascontiguousarray(np.asarray(inp["L1_attn_gn_weight"], f32).reshape(1, D)),
               gnb=np.ascontiguousarray(np.asarray(inp["L1_attn_gn_bias"], f32).reshape(1, D)),
               woT=np.ascontiguousarray(np.asarray(inp["L1_attn_w_output"], f32).T),
               identb=consts['identb'], ones1f=consts['ones1f'])
    ws = _shards(wkv)
    gs = _shards(gf)
    res = _run(nc_o, [dict(wkvi=ws[c], gi=gs[c], x4=x4s[c], **com) for c in range(NCORE)])
    x5 = _gather(res, "x5")
    _cb("x5", x5)

    x6 = launch_ffn(x5, l1_wT, l1_gwT, l1_acatT, l1_bcatT, l1_eb, l1_bb, cfg1, consts)
    _cb("x6", x6)
    return x6.reshape(2, 4096, D)


# version 05 (final)
# speedup vs baseline: 1.5998x; 1.0770x over previous
import contextlib
import numpy as np
import ml_dtypes
import concourse.bass as bass
import concourse.mybir as mybir
from concourse.bass_utils import run_bass_kernel_spmd

F32 = mybir.dt.float32
BF16 = mybir.dt.bfloat16
AF = mybir.ActivationFunctionType
AL = mybir.AluOpType
AX = mybir.AxisListType.X
BF = ml_dtypes.bfloat16

NCORE = 8
TOK = 1024
NT = 8
D = 4096
KD = 32


class Em:
    def __init__(self):
        self.ops = []

    def op(self, eng, fn, inc=1):
        self.ops.append((eng, fn, inc))

    def v(self, fn):
        self.op('vector', fn, 1)

    def s(self, fn):
        self.op('scalar', fn, 1)

    def t(self, fn):
        self.op('tensor', fn, 1)

    def dma(self, out, in_, **kw):
        self.op('sync', lambda e, o=out, i=in_, k=kw: e.dma_start(o, i, **k), 16)

    def build(self, nc, sem):
        engines = ['sync', 'vector', 'scalar', 'tensor', 'gpsimd']
        n = len(self.ops)
        wait_flag = [True] * n
        inc_amt = [inc for (_, _, inc) in self.ops]
        i = 0
        while i < n:
            if self.ops[i][0] == 'tensor':
                j = i
                while j + 1 < n and self.ops[j + 1][0] == 'tensor':
                    j += 1
                if j > i:
                    tot = sum(inc_amt[i:j + 1])
                    for k in range(i, j + 1):
                        inc_amt[k] = 0
                        if k > i:
                            wait_flag[k] = False
                    inc_amt[j] = tot
                i = j + 1
            else:
                i += 1
        with nc.Block() as block:
            def mk(name):
                def f(eng):
                    cum = 0
                    for idx, (ename, fn, inc) in enumerate(self.ops):
                        if ename == name:
                            if cum > 0 and wait_flag[idx]:
                                eng.wait_ge(sem, cum)
                            inst = fn(eng)
                            if inc_amt[idx] > 0:
                                inst.then_inc(sem, inc_amt[idx])
                        cum += inc
                    eng.wait_ge(sem, cum)
                return f
            for name in engines:
                if any(e == name for (e, _, _) in self.ops):
                    getattr(block, name)(mk(name))


class B:
    def __init__(self):
        self.nc = bass.Bass("TRN2", target_bir_lowering=False)
        self.em = Em()
        self.stack = contextlib.ExitStack()
        self._n = 0

    def inp(self, name, shape, dt=F32):
        return self.nc.dram_tensor(name, list(shape), dt, kind="ExternalInput").ap()

    def outp(self, name, shape, dt=F32):
        return self.nc.dram_tensor(name, list(shape), dt, kind="ExternalOutput").ap()

    def scr(self, name, shape, dt=F32):
        return self.nc.dram_tensor(name, list(shape), dt, kind="Internal").ap()

    def sb(self, shape, dt=F32):
        self._n += 1
        return self.stack.enter_context(self.nc.sbuf_tensor(f"sb{self._n}", list(shape), dt))

    def ps(self, shape, dt=F32):
        self._n += 1
        return self.stack.enter_context(self.nc.psum_tensor(f"ps{self._n}", list(shape), dt))

    @staticmethod
    def _a(x):
        return x[:] if hasattr(x, 'alloc') or type(x).__name__ == 'SBTensorHandle' else x

    def tt(self, out, a, b_, op):
        out, a, b_ = B._a(out), B._a(a), B._a(b_)
        self.em.v(lambda e, o=out, x=a, y=b_, p=op: e.tensor_tensor(o, x, y, p))

    def ts(self, out, a, s1, s2, op0, op1=None):
        out, a = B._a(out), B._a(a)
        s1 = B._a(s1) if not isinstance(s1, (int, float)) else s1
        if s2 is not None and not isinstance(s2, (int, float)):
            s2 = B._a(s2)
        if op1 is None:
            self.em.v(lambda e, o=out, x=a, u=s1, p=op0: e.tensor_scalar(o, x, u, None, p))
        else:
            self.em.v(lambda e, o=out, x=a, u=s1, w=s2, p=op0, q=op1:
                      e.tensor_scalar(o, x, u, w, p, q))

    def stt(self, out, in0, scalar, in1, op0, op1):
        out, in0, in1 = B._a(out), B._a(in0), B._a(in1)
        if not isinstance(scalar, (int, float)):
            scalar = B._a(scalar)
        self.em.v(lambda e, o=out, x=in0, s=scalar, y=in1, p=op0, q=op1:
                  e.scalar_tensor_tensor(o, x, s, y, p, q))

    def red(self, out, a, op):
        out, a = B._a(out), B._a(a)
        self.em.v(lambda e, o=out, x=a, p=op: e.tensor_reduce(o, x, AX, p))

    def act(self, out, a, func, bias=None, scale=None, accum=None):
        out, a = B._a(out), B._a(a)
        if bias is not None and not isinstance(bias, (int, float)):
            bias = B._a(bias)
        if accum is not None:
            accum = B._a(accum)
        kw = {}
        if bias is not None:
            kw['bias'] = bias
        if scale is not None:
            kw['scale'] = scale
        if accum is not None:
            kw['accum_out'] = accum
        self.em.s(lambda e, o=out, x=a, f=func, k=kw: e.activation(o, x, f, **k))

    def vcopy(self, out, a):
        out, a = B._a(out), B._a(a)
        self.em.v(lambda e, o=out, x=a: e.tensor_copy(o, x))

    def scopy(self, out, a):
        out, a = B._a(out), B._a(a)
        self.em.s(lambda e, o=out, x=a: e.activation(o, x, AF.Copy))

    def memset(self, ap, c):
        ap = B._a(ap)
        self.em.v(lambda e, o=ap, v=c: e.memset(o, v))

    def recip(self, out, a):
        out, a = B._a(out), B._a(a)
        self.em.v(lambda e, o=out, x=a: e.reciprocal(o, x))

    def scan(self, out, d0, d1, init):
        out, d0, d1, init = B._a(out), B._a(d0), B._a(d1), B._a(init)
        self.em.v(lambda e, o=out, a=d0, b_=d1, i=init:
                  e.tensor_tensor_scan(o, a, b_, i, AL.mult, AL.add))

    def mm(self, out, lhsT, rhs, start, stop):
        out, lhsT, rhs = B._a(out), B._a(lhsT), B._a(rhs)
        self.em.t(lambda e, o=out, l=lhsT, r=rhs, s1=start, s2=stop:
                  e.matmul(o, l, r, start=s1, stop=s2))

    def tr(self, out, a, ident):
        out, a, ident = B._a(out), B._a(a), B._a(ident)
        self.em.t(lambda e, o=out, x=a, i=ident: e.transpose(o, x, i))

    def dma(self, out, in_, **kw):
        self.em.dma(B._a(out), B._a(in_), **kw)

    def fin(self):
        sem_cm = self.nc.semaphore()
        sem = sem_cm.__enter__()
        self.em.build(self.nc, sem)
        sem_cm.__exit__(None, None, None)
        self.stack.close()
        return self.nc


def ln_tile(b, src, ncols, cols, out, wbc=None, bbc=None, act=None, xc=None, sq=None):
    mu, vs, t0, rstd = cols['mu'], cols['vs'], cols['t0'], cols['rstd']
    b.red(mu, src, AL.add)
    b.ts(mu, mu, 1.0 / ncols, None, AL.mult)
    b.ts(xc, src, mu[:, 0:1], None, AL.subtract)
    b.act(sq, xc, AF.Square, accum=vs)
    b.ts(t0, vs, 1.0 / ncols, 1e-5, AL.mult, AL.add)
    b.act(t0, t0, AF.Ln)
    b.act(rstd, t0, AF.Exp, scale=-0.5)
    if wbc is None:
        b.ts(out, xc, rstd[:, 0:1], None, AL.mult)
    else:
        b.ts(xc, xc, rstd[:, 0:1], None, AL.mult)
        b.tt(xc, xc, wbc, AL.mult)
        if act is None:
            b.tt(out, xc, bbc, AL.add)
        else:
            b.tt(xc, xc, bbc, AL.add)
            b.act(out, xc, act)


def mk_cols(b, P=128):
    return dict(mu=b.sb([P, 1]), vs=b.sb([P, 1]), t0=b.sb([P, 1]), rstd=b.sb([P, 1]))


def bcast_row(b, ps_bc, row_sb, onesf, dst, n):
    for o in range(0, n, 512):
        w = min(512, n - o)
        b.mm(ps_bc[:, 0:w], onesf[0:1, :], row_sb[0:1, o:o + w], True, True)
        b.vcopy(dst[:, o:o + w], ps_bc[:, 0:w])


def build_ffn(E, R, topk, lora_scale, gate_bias):
    ER = E * R
    b = B()
    xin = b.inp("xin", [TOK, D])
    wT = b.inp("wT", [D, D])
    gwT = b.inp("gwT", [D, E])
    acatT = b.inp("acatT", [D, ER])
    bcatT = b.inp("bcatT", [ER, D])
    eb = b.inp("eb", [1, E])
    bb = b.inp("bb", [1, D])
    identb = b.inp("identb", [128, 128], BF16)
    ones1f = b.inp("ones1f", [1, 128])
    ones1b = b.inp("ones1b", [1, 128], BF16)
    xout = b.outp("xout", [TOK, D])

    uT = b.sb([128, KD * TOK], BF16)
    xt = b.sb([128, D])
    xc = b.sb([128, D])
    sq = b.sb([128, D])
    u_bf = b.sb([128, D], BF16)
    cols = mk_cols(b)
    idb = b.sb([128, 128], BF16)
    o1f = b.sb([1, 128])
    o1b = b.sb([1, 128], BF16)
    gw_bf = b.sb([128, KD * E], BF16)
    acat_bf = b.sb([128, KD * ER], BF16)
    bcat_bf = b.sb([ER, D], BF16)
    eb_bf = b.sb([1, E], BF16)
    bb_bf = b.sb([1, D], BF16)
    tmp_row = b.sb([1, D])
    logits = b.sb([128, E])
    m1 = b.sb([128, 1])
    m2 = b.sb([128, 1])
    s1 = b.sb([128, 1])
    s2 = b.sb([128, 1])
    oh1 = b.sb([128, E])
    oh2 = b.sb([128, E])
    msk = b.sb([128, E])
    wfull = b.sb([128, E])
    zw = b.sb([128, ER], BF16)
    zwT = b.sb([ER, TOK], BF16)

    tr_ps = b.ps([128, 128], BF16)
    sm_ps = b.ps([128, 128])
    mains = [b.ps([128, 512]) for _ in range(4)]
    wbuf = xt[:, 0:2048]
    w_bf = xc[:, 0:1024].bitcast(BF16)
    xres = xc[:, 2048:2560]
    o_sb = xc[:, 3072:3584]

    b.dma(idb[:], identb)
    b.dma(o1f[:], ones1f)
    b.dma(o1b[:], ones1b)
    tmp_eb = b.sb([1, E])
    b.dma(tmp_eb[:], eb)
    b.scopy(eb_bf[:], tmp_eb[:])
    b.dma(tmp_row[:], bb)
    b.scopy(bb_bf[:], tmp_row[:])

    for t in range(NT):
        b.dma(xt[:], xin[t * 128:(t + 1) * 128, :])
        ln_tile(b, xt[:], D, cols, u_bf[:], xc=xc[:], sq=sq[:])
        for k in range(KD):
            b.tr(tr_ps[:], u_bf[:, k * 128:(k + 1) * 128], idb[:])
            b.vcopy(uT[:, k * TOK + t * 128: k * TOK + (t + 1) * 128], tr_ps[:])

    b.dma(sq[:, 0:KD * E].rearrange("p (k e) -> p k e", k=KD),
          gwT.rearrange("(k p) e -> p k e", p=128))
    b.scopy(gw_bf[:], sq[:, 0:KD * E])
    b.dma(sq[:, 0:KD * ER].rearrange("p (k e) -> p k e", k=KD),
          acatT.rearrange("(k p) e -> p k e", p=128))
    b.scopy(acat_bf[:], sq[:, 0:KD * ER])
    b.dma(sq[0:ER, 0:D], bcatT)
    b.scopy(bcat_bf[:], sq[0:ER, 0:D])

    def lhs(k, t):
        return uT[:, k * TOK + t * 128: k * TOK + (t + 1) * 128]

    for t in range(NT):
        for k in range(KD):
            b.mm(sm_ps[:, 0:E], lhs(k, t), gw_bf[:, k * E:(k + 1) * E], k == 0, False)
        b.mm(sm_ps[:, 0:E], o1b[0:1, :], eb_bf[0:1, :], False, True)
        b.vcopy(logits[:], sm_ps[:, 0:E])
        b.red(m1, logits[:], AL.max)
        b.ts(oh1[:], logits[:], m1[:, 0:1], None, AL.is_ge)
        if topk == 1:
            b.vcopy(wfull[:], oh1[:])
        else:
            b.stt(msk[:], oh1[:], -1e30, logits[:], AL.mult, AL.add)
            b.red(m2, msk[:], AL.max)
            b.ts(oh2[:], msk[:], m2[:, 0:1], None, AL.is_ge)
            b.ts(s1, m1[:], m2[:, 0:1], None, AL.subtract)
            b.act(s2, s1, AF.Sigmoid, scale=-1.0)
            b.act(s1, s1, AF.Sigmoid)
            b.ts(wfull[:], oh1[:], s1[:, 0:1], None, AL.mult)
            b.stt(wfull[:], oh2[:], s2[:, 0:1], wfull[:], AL.mult, AL.add)
        if lora_scale != 1.0:
            b.ts(wfull[:], wfull[:], float(lora_scale), None, AL.mult)
        for k in range(KD):
            b.mm(sm_ps[:, 0:ER], lhs(k, t), acat_bf[:, k * ER:(k + 1) * ER], k == 0, k == KD - 1)
        b.tt(zw[:].rearrange("p (e r) -> p e r", e=E),
             sm_ps[:, 0:ER].rearrange("p (e r) -> p e r", e=E),
             wfull[:].unsqueeze(2).broadcast_to([128, E, R]), AL.mult)
        b.tr(tr_ps[0:ER, :], zw[:], idb[:])
        b.vcopy(zwT[:, t * 128:(t + 1) * 128], tr_ps[0:ER, :])

    for tg in range(2):
        tl = list(range(tg * 4, tg * 4 + 4))
        for ob in range(8):
            for kg in range(KD // 4):
                b.dma(wbuf.rearrange("p (k o) -> p k o", k=4),
                      wT[kg * 512:(kg + 1) * 512, ob * 512:(ob + 1) * 512]
                      .rearrange("(k p) o -> p k o", p=128))
                b.scopy(w_bf, wbuf)
                for kk in range(4):
                    k = kg * 4 + kk
                    for i, t in enumerate(tl):
                        b.mm(mains[i][:], lhs(k, t), w_bf[:, kk * 512:(kk + 1) * 512], k == 0, False)
            for i, t in enumerate(tl):
                b.mm(mains[i][:], zwT[:, t * 128:(t + 1) * 128], bcat_bf[:, ob * 512:(ob + 1) * 512], False, False)
                b.mm(mains[i][:], o1b[0:1, :], bb_bf[0:1, ob * 512:(ob + 1) * 512], False, True)
            for i, t in enumerate(tl):
                b.dma(xres, xin[t * 128:(t + 1) * 128, ob * 512:(ob + 1) * 512])
                b.stt(o_sb, mains[i][:], 0.5, xres, AL.mult, AL.add)
                b.dma(xout[t * 128:(t + 1) * 128, ob * 512:(ob + 1) * 512], o_sb)
    return b.fin()


def build_attn0_in():
    DI = 512
    b = B()
    xin = b.inp("xin", [TOK, D])
    winT = b.inp("winT", [D, DI])
    binr = b.inp("binr", [1, DI])
    ln1w = b.inp("ln1w", [1, DI])
    ln1b = b.inp("ln1b", [1, DI])
    identb = b.inp("identb", [128, 128], BF16)
    ones1f = b.inp("ones1f", [1, 128])
    ones1b = b.inp("ones1b", [1, 128], BF16)
    h0 = b.outp("h0", [TOK, DI])
    y0 = b.outp("y0", [TOK, DI])

    uT = b.sb([128, KD * TOK], BF16)
    xt = b.sb([128, D])
    xc = b.sb([128, D])
    sq = b.sb([128, D])
    u_bf = b.sb([128, D], BF16)
    cols = mk_cols(b)
    idb = b.sb([128, 128], BF16)
    o1f = b.sb([1, 128])
    o1b = b.sb([1, 128], BF16)
    bin_bf = b.sb([1, DI], BF16)
    tmp_row = b.sb([1, DI])
    wbuf = b.sb([128, 2048])
    w_bf = b.sb([128, 2048], BF16)
    wbc = b.sb([128, DI])
    bbc = b.sb([128, DI])
    h_sb = [b.sb([128, DI]) for _ in range(NT)]
    y_sb = b.sb([128, DI])
    xc2 = b.sb([128, DI])
    sq2 = b.sb([128, DI])

    tr_ps = b.ps([128, 128], BF16)
    bc_ps = b.ps([128, 512])
    mains = [b.ps([128, 512]) for _ in range(4)]

    b.dma(idb[:], identb)
    b.dma(o1f[:], ones1f)
    b.dma(o1b[:], ones1b)
    b.dma(tmp_row[:], binr)
    b.scopy(bin_bf[:], tmp_row[:])
    b.dma(tmp_row[:], ln1w)
    bcast_row(b, bc_ps, tmp_row, o1f, wbc, DI)
    b.dma(tmp_row[:], ln1b)
    bcast_row(b, bc_ps, tmp_row, o1f, bbc, DI)

    for t in range(NT):
        b.dma(xt[:], xin[t * 128:(t + 1) * 128, :])
        ln_tile(b, xt[:], D, cols, u_bf[:], xc=xc[:], sq=sq[:])
        for k in range(KD):
            b.tr(tr_ps[:], u_bf[:, k * 128:(k + 1) * 128], idb[:])
            b.vcopy(uT[:, k * TOK + t * 128: k * TOK + (t + 1) * 128], tr_ps[:])

    for tg in range(2):
        tl = list(range(tg * 4, tg * 4 + 4))
        for kg in range(KD // 4):
            b.dma(wbuf[:].rearrange("p (k o) -> p k o", k=4),
                  winT[kg * 512:(kg + 1) * 512, :].rearrange("(k p) o -> p k o", p=128))
            b.scopy(w_bf[:], wbuf[:])
            for kk in range(4):
                k = kg * 4 + kk
                for i, t in enumerate(tl):
                    b.mm(mains[i][:], uT[:, k * TOK + t * 128: k * TOK + (t + 1) * 128],
                         w_bf[:, kk * 512:(kk + 1) * 512], k == 0, False)
        for i, t in enumerate(tl):
            b.mm(mains[i][:], o1b[0:1, :], bin_bf[0:1, :], False, True)
            b.vcopy(h_sb[t][:], mains[i][:])
            b.dma(h0[t * 128:(t + 1) * 128, :], h_sb[t][:])
    for t in range(NT):
        ln_tile(b, h_sb[t][:], DI, cols, y_sb[:], wbc=wbc[:], bbc=bbc[:], act=AF.Silu,
                xc=xc2[:], sq=sq2[:])
        b.dma(y0[t * 128:(t + 1) * 128, :], y_sb[:])
    return b.fin()


def build_scan():
    S = 4096
    b = B()
    ycs = b.inp("ycs", [S, 128])
    phr = b.inp("phr", [128, 1])
    phi = b.inp("phi", [128, 1])
    lcr = b.inp("lcr", [128, 1])
    lci = b.inp("lci", [128, 1])
    identf = b.inp("identf", [128, 128])
    convo = b.outp("convo", [S, 128])

    y_sb = b.sb([128, S])
    yT = b.sb([128, S])
    C = b.sb([128, S])
    Sn = b.sb([128, S])
    A = b.sb([128, S])
    Bb = b.sb([128, S])
    RHO = b.sb([128, S])
    tmp1 = b.sb([128, 2048])
    tmp2 = b.sb([128, 2048])
    idf = b.sb([128, 128])
    pr = b.sb([128, 1])
    pi = b.sb([128, 1])
    r2 = b.sb([128, 1])
    rr = b.sb([128, 1])
    th = b.sb([128, 1])
    ir = b.sb([128, 1])
    ere = b.sb([128, 1])
    eim = b.sb([128, 1])
    cn = b.sb([128, 1])
    sn = b.sb([128, 1])
    t1c = b.sb([128, 1])
    t2c = b.sb([128, 1])
    a0 = b.sb([128, 1])
    b0 = b.sb([128, 1])
    tr_ps = b.ps([128, 128])

    b.dma(idf[:], identf)
    b.dma(y_sb[:].rearrange("p (t c) -> p t c", t=KD), ycs.rearrange("(t p) c -> p t c", p=128))
    b.dma(pr[:], phr)
    b.dma(pi[:], phi)
    b.dma(a0[:], lcr)
    b.dma(b0[:], lci)
    for t in range(KD):
        b.tr(tr_ps[:], y_sb[:, t * 128:(t + 1) * 128], idf[:])
        b.vcopy(yT[:, t * 128:(t + 1) * 128], tr_ps[:])

    b.tt(r2[:], pr[:], pr[:], AL.mult)
    b.tt(t1c[:], pi[:], pi[:], AL.mult)
    b.tt(r2[:], r2[:], t1c[:], AL.add)
    b.act(rr, r2, AF.Sqrt)
    b.act(th, rr, AF.Tanh)
    b.recip(ir[:], rr[:])
    b.tt(ere[:], pr[:], ir[:], AL.mult)
    b.tt(eim[:], pi[:], ir[:], AL.mult)

    b.memset(C[:, 0:1], 1.0)
    b.memset(Sn[:, 0:1], 0.0)
    b.vcopy(cn[:], ere[:])
    b.vcopy(sn[:], eim[:])
    n = 1
    while n < S:
        if n > 1:
            b.tt(t1c[:], C[:, n - 1:n], ere[:], AL.mult)
            b.tt(t2c[:], Sn[:, n - 1:n], eim[:], AL.mult)
            b.tt(cn[:], t1c[:], t2c[:], AL.subtract)
            b.tt(t1c[:], C[:, n - 1:n], eim[:], AL.mult)
            b.tt(t2c[:], Sn[:, n - 1:n], ere[:], AL.mult)
            b.tt(sn[:], t1c[:], t2c[:], AL.add)
        m = min(n, S - n)
        b.ts(tmp1[:, 0:m], C[:, 0:m], cn[:, 0:1], None, AL.mult)
        b.ts(tmp2[:, 0:m], Sn[:, 0:m], sn[:, 0:1], None, AL.mult)
        b.tt(C[:, n:n + m], tmp1[:, 0:m], tmp2[:, 0:m], AL.subtract)
        b.ts(tmp1[:, 0:m], Sn[:, 0:m], cn[:, 0:1], None, AL.mult)
        b.ts(tmp2[:, 0:m], C[:, 0:m], sn[:, 0:1], None, AL.mult)
        b.tt(Sn[:, n:n + m], tmp1[:, 0:m], tmp2[:, 0:m], AL.add)
        n *= 2

    b.tt(t1c[:], a0[:], ere[:], AL.mult)
    b.tt(t2c[:], b0[:], eim[:], AL.mult)
    b.tt(r2[:], a0[:], eim[:], AL.mult)
    b.tt(ir[:], b0[:], ere[:], AL.mult)
    b.tt(a0[:], t1c[:], t2c[:], AL.subtract)
    b.tt(b0[:], r2[:], ir[:], AL.add)
    b.ts(b0[:], b0[:], -1.0, None, AL.mult)

    b.memset(RHO[:], 1.0)
    b.ts(RHO[:], RHO[:], th[:, 0:1], None, AL.mult)

    half = 2048
    b.tt(y_sb[:], yT[:], C[:], AL.mult)
    b.scan(A[:, 0:half], RHO[:, 0:half], y_sb[:, 0:half], a0[:, 0:1])
    b.scan(A[:, half:S], RHO[:, half:S], y_sb[:, half:S], A[:, half - 1:half])
    b.tt(y_sb[:], yT[:], Sn[:], AL.mult)
    b.scan(Bb[:, 0:half], RHO[:, 0:half], y_sb[:, 0:half], b0[:, 0:1])
    b.scan(Bb[:, half:S], RHO[:, half:S], y_sb[:, half:S], Bb[:, half - 1:half])

    b.tt(A[:], C[:], A[:], AL.mult)
    b.tt(Bb[:], Sn[:], Bb[:], AL.mult)
    b.tt(A[:], A[:], Bb[:], AL.add)

    for t in range(KD):
        b.tr(tr_ps[:], A[:, t * 128:(t + 1) * 128], idf[:])
        b.vcopy(y_sb[:, t * 128:(t + 1) * 128], tr_ps[:])
    b.dma(convo.rearrange("(t p) c -> p t c", p=128), y_sb[:].rearrange("p (t c) -> p t c", t=KD))
    return b.fin()


def build_mlp():
    DI, DF = 512, 2048
    b = B()
    hin = b.inp("hin", [TOK, DI])
    cin = b.inp("cin", [TOK, DI])
    sclr = b.inp("sclr", [1, DI])
    ln2w = b.inp("ln2w", [1, DI])
    ln2b = b.inp("ln2b", [1, DI])
    w1T = b.inp("w1T", [DI, DF])
    b1r = b.inp("b1r", [1, DF])
    w2T = b.inp("w2T", [DF, DI])
    b2r = b.inp("b2r", [1, DI])
    lnnw = b.inp("lnnw", [1, DI])
    lnnb = b.inp("lnnb", [1, DI])
    identb = b.inp("identb", [128, 128], BF16)
    ones1f = b.inp("ones1f", [1, 128])
    ones1b = b.inp("ones1b", [1, 128], BF16)
    hout = b.outp("hout", [TOK, DI])
    ynext = b.outp("ynext", [TOK, DI])

    idb = b.sb([128, 128], BF16)
    o1f = b.sb([1, 128])
    o1b = b.sb([1, 128], BF16)
    cols = mk_cols(b)
    h_t = b.sb([128, DI])
    c_t = b.sb([128, DI])
    xc = b.sb([128, DF])
    sq = b.sb([128, DF])
    sclbc = b.sb([128, DI])
    w2bc = b.sb([128, DI])
    b2bc = b.sb([128, DI])
    wnbc = b.sb([128, DI])
    bnbc = b.sb([128, DI])
    tmp_row = b.sb([1, DF])
    z_bf = b.sb([128, DI], BF16)
    zT = b.sb([128, DI], BF16)
    w1_sb = b.sb([128, 4 * DF])
    w1_bf = b.sb([128, 4 * DF], BF16)
    w2_sb = b.sb([128, 16 * DI])
    w2_bf = b.sb([128, 16 * DI], BF16)
    b1_bf = b.sb([1, DF], BF16)
    b2_bf = b.sb([1, DI], BF16)
    f_bf = b.sb([128, DF], BF16)
    fT = b.sb([128, DF], BF16)
    y_sb = b.sb([128, DI])

    tr_ps = b.ps([128, 128], BF16)
    bc_ps = b.ps([128, 512])
    mainA = b.ps([128, 512])
    mainB = b.ps([128, 512])

    b.dma(idb[:], identb)
    b.dma(o1f[:], ones1f)
    b.dma(o1b[:], ones1b)
    for (row, dst) in [(sclr, sclbc), (ln2w, w2bc), (ln2b, b2bc), (lnnw, wnbc), (lnnb, bnbc)]:
        b.dma(tmp_row[0:1, 0:DI], row)
        bcast_row(b, bc_ps, tmp_row[0:1, 0:DI], o1f, dst, DI)
    b.dma(w1_sb[:].rearrange("p (k f) -> p k f", k=4), w1T.rearrange("(k p) f -> p k f", p=128))
    b.scopy(w1_bf[:], w1_sb[:])
    b.dma(w2_sb[:].rearrange("p (k f) -> p k f", k=16), w2T.rearrange("(k p) f -> p k f", p=128))
    b.scopy(w2_bf[:], w2_sb[:])
    b.dma(tmp_row[:], b1r)
    b.scopy(b1_bf[:], tmp_row[:])
    b.dma(tmp_row[0:1, 0:DI], b2r)
    b.scopy(b2_bf[:], tmp_row[0:1, 0:DI])

    for t in range(NT):
        b.dma(h_t[:], hin[t * 128:(t + 1) * 128, :])
        b.dma(c_t[:], cin[t * 128:(t + 1) * 128, :])
        b.tt(c_t[:], c_t[:], sclbc[:], AL.mult)
        b.tt(h_t[:], h_t[:], c_t[:], AL.add)
        ln_tile(b, h_t[:], DI, cols, z_bf[:], wbc=w2bc[:], bbc=b2bc[:],
                xc=xc[:, 0:DI], sq=sq[:, 0:DI])
        for k in range(4):
            b.tr(tr_ps[:], z_bf[:, k * 128:(k + 1) * 128], idb[:])
            b.vcopy(zT[:, k * 128:(k + 1) * 128], tr_ps[:])
        for o in range(4):
            for k in range(4):
                b.mm(mainA[:], zT[:, k * 128:(k + 1) * 128],
                     w1_bf[:, k * DF + o * 512: k * DF + (o + 1) * 512], k == 0, False)
            b.mm(mainA[:], o1b[0:1, :], b1_bf[0:1, o * 512:(o + 1) * 512], False, True)
            b.act(f_bf[:, o * 512:(o + 1) * 512], mainA[:], AF.Silu)
        for k in range(16):
            b.tr(tr_ps[:], f_bf[:, k * 128:(k + 1) * 128], idb[:])
            b.vcopy(fT[:, k * 128:(k + 1) * 128], tr_ps[:])
        for k in range(16):
            b.mm(mainB[:], fT[:, k * 128:(k + 1) * 128],
                 w2_bf[:, k * DI:(k + 1) * DI], k == 0, False)
        b.mm(mainB[:], o1b[0:1, :], b2_bf[0:1, :], False, True)
        b.tt(h_t[:], h_t[:], mainB[:], AL.add)
        b.dma(hout[t * 128:(t + 1) * 128, :], h_t[:])
        ln_tile(b, h_t[:], DI, cols, y_sb[:], wbc=wnbc[:], bbc=bnbc[:], act=AF.Silu,
                xc=xc[:, 0:DI], sq=sq[:, 0:DI])
        b.dma(ynext[t * 128:(t + 1) * 128, :], y_sb[:])
    return b.fin()


def build_attn0_final():
    DI = 512
    b = B()
    hin = b.inp("hin", [TOK, DI])
    x1 = b.inp("x1", [TOK, D])
    lnlw = b.inp("lnlw", [1, DI])
    lnlb = b.inp("lnlb", [1, DI])
    woutT = b.inp("woutT", [DI, D])
    boutr = b.inp("boutr", [1, D])
    identb = b.inp("identb", [128, 128], BF16)
    ones1f = b.inp("ones1f", [1, 128])
    ones1b = b.inp("ones1b", [1, 128], BF16)
    x2 = b.outp("x2", [TOK, D])

    idb = b.sb([128, 128], BF16)
    o1f = b.sb([1, 128])
    o1b = b.sb([1, 128], BF16)
    cols = mk_cols(b)
    h_t = b.sb([128, DI])
    xc = b.sb([128, DI])
    sq = b.sb([128, DI])
    wbc = b.sb([128, DI])
    bbc = b.sb([128, DI])
    tmp_row = b.sb([1, D])
    hf_bf = b.sb([128, DI], BF16)
    hT = b.sb([128, DI], BF16)
    bout_bf = b.sb([1, D], BF16)
    xres = b.sb([128, 512])
    o_sb = b.sb([128, 512])

    tr_ps = b.ps([128, 128], BF16)
    bc_ps = b.ps([128, 512])
    main = b.ps([128, 512])

    b.dma(idb[:], identb)
    b.dma(o1f[:], ones1f)
    b.dma(o1b[:], ones1b)
    b.dma(tmp_row[0:1, 0:DI], lnlw)
    bcast_row(b, bc_ps, tmp_row[0:1, 0:DI], o1f, wbc, DI)
    b.dma(tmp_row[0:1, 0:DI], lnlb)
    bcast_row(b, bc_ps, tmp_row[0:1, 0:DI], o1f, bbc, DI)
    b.dma(tmp_row[:], boutr)
    b.scopy(bout_bf[:], tmp_row[:])
    w_f32 = b.sb([128, D])
    wout_bf = b.sb([128, 4 * D], BF16)
    for k in range(4):
        b.dma(w_f32[:], woutT[k * 128:(k + 1) * 128, :])
        b.scopy(wout_bf[:, k * D:(k + 1) * D], w_f32[:])

    for t in range(NT):
        b.dma(h_t[:], hin[t * 128:(t + 1) * 128, :])
        ln_tile(b, h_t[:], DI, cols, hf_bf[:], wbc=wbc[:], bbc=bbc[:], xc=xc[:], sq=sq[:])
        for k in range(4):
            b.tr(tr_ps[:], hf_bf[:, k * 128:(k + 1) * 128], idb[:])
            b.vcopy(hT[:, k * 128:(k + 1) * 128], tr_ps[:])
        for ob in range(8):
            for k in range(4):
                b.mm(main[:], hT[:, k * 128:(k + 1) * 128],
                     wout_bf[:, k * D + ob * 512: k * D + (ob + 1) * 512], k == 0, False)
            b.mm(main[:], o1b[0:1, :], bout_bf[0:1, ob * 512:(ob + 1) * 512], False, True)
            b.dma(xres[:], x1[t * 128:(t + 1) * 128, ob * 512:(ob + 1) * 512])
            b.tt(o_sb[:], main[:], xres[:], AL.add)
            b.dma(x2[t * 128:(t + 1) * 128, ob * 512:(ob + 1) * 512], o_sb[:])
    return b.fin()


def build_rwkv_proj():
    b = B()
    xin = b.inp("xin", [TOK, D])
    halo = b.inp("halo", [1, D])
    wts = [b.inp(nm, [D, D]) for nm in ["wkT", "wvT", "wrT", "wgT"]]
    tms = [b.inp(nm, [128, KD]) for nm in ["tmk", "tmv", "tmr", "tmg"]]
    identf = b.inp("identf", [128, 128])
    outs = [b.outp(nm, [TOK, D]) for nm in ["ko", "vo", "ro", "go"]]
    hscr = b.scr("hscr", [1, D])

    idf = b.sb([128, 128])
    cols = mk_cols(b)
    xt = b.sb([128, D])
    xc = b.sb([128, D])
    uTall = b.sb([128, KD * (TOK + 1)], BF16)
    mixT = b.sb([128, KD * TOK], BF16)
    tm_sb = [b.sb([128, KD]) for _ in range(4)]
    om_sb = [b.sb([128, KD]) for _ in range(4)]
    halocols = b.sb([128, KD])
    hcols = mk_cols(b, 1)
    mtmp = b.sb([128, 128])

    mains = [b.ps([128, 512]) for _ in range(8)]
    wbuf = xt[:, 0:2048]
    w_bf = xc[:, 0:1024].bitcast(BF16)
    o_sb = xc[:, 2048:2560]

    b.dma(idf[:], identf)
    for m in range(4):
        b.dma(tm_sb[m][:], tms[m])
        b.ts(om_sb[m][:], tm_sb[m][:], -1.0, 1.0, AL.mult, AL.add)

    b.dma(xt[0:1, :], halo)
    ln_tile(b, xt[0:1, :], D, hcols, xc[0:1, :], xc=xc[0:1, :], sq=xt[0:1, :])
    b.dma(hscr, xc[0:1, :])
    b.dma(halocols[:], hscr.rearrange("o (f p) -> (o p) f", p=128),
          allow_slow_non_contiguous=True)
    for k in range(KD):
        b.scopy(uTall[:, k * (TOK + 1):k * (TOK + 1) + 1], halocols[:, k:k + 1])

    base = lambda k: k * (TOK + 1)
    for t in range(NT):
        b.dma(xt[:], xin[t * 128:(t + 1) * 128, :])
        ln_tile(b, xt[:], D, cols, xc[:], xc=xc[:], sq=xt[:])
        for k in range(KD):
            b.tr(mains[0][:, 0:128], xc[:, k * 128:(k + 1) * 128], idf[:])
            b.scopy(uTall[:, base(k) + 1 + t * 128: base(k) + 1 + (t + 1) * 128],
                    mains[0][:, 0:128])

    for m in range(4):
        for k in range(KD):
            for t in range(NT):
                cur = uTall[:, base(k) + 1 + t * 128: base(k) + 1 + (t + 1) * 128]
                sh = uTall[:, base(k) + t * 128: base(k) + (t + 1) * 128]
                b.ts(mtmp[:], sh, om_sb[m][:, k:k + 1], None, AL.mult)
                b.stt(mixT[:, k * TOK + t * 128: k * TOK + (t + 1) * 128],
                      cur, tm_sb[m][:, k:k + 1], mtmp[:], AL.mult, AL.add)
        for ob in range(8):
            for kg in range(KD // 4):
                b.dma(wbuf.rearrange("p (k o) -> p k o", k=4),
                      wts[m][kg * 512:(kg + 1) * 512, ob * 512:(ob + 1) * 512]
                      .rearrange("(k p) o -> p k o", p=128))
                b.scopy(w_bf, wbuf)
                for kk in range(4):
                    k = kg * 4 + kk
                    for t in range(NT):
                        b.mm(mains[t][:], mixT[:, k * TOK + t * 128: k * TOK + (t + 1) * 128],
                             w_bf[:, kk * 512:(kk + 1) * 512], k == 0, k == KD - 1)
            for t in range(NT):
                if m == 3:
                    b.act(o_sb, mains[t][:], AF.Silu)
                else:
                    b.vcopy(o_sb, mains[t][:])
                b.dma(outs[m][t * 128:(t + 1) * 128, ob * 512:(ob + 1) * 512], o_sb)
    return b.fin()


def build_wkv():
    S, HD, NH, C = 4096, 64, 8, 128
    NJ = S // C
    b = B()
    kk = b.inp("kk", [2, S, NH * HD])
    vv = b.inp("vv", [2, S, NH * HD])
    rr = b.inp("rr", [2, S, NH * HD])
    td8 = b.inp("td8", [1, NH * HD])
    tf8 = b.inp("tf8", [1, NH * HD])
    wexp = b.inp("wexp", [1, 6 * 128])
    offpad = b.inp("offpad", [1, 784])
    mkeep = b.inp("mkeep", [128, 128])
    m16off = b.inp("m16off", [128, 128])
    mdiag = b.inp("mdiag", [128, 128])
    identf = b.inp("identf", [128, 128])
    ones1f = b.inp("ones1f", [1, 128])
    wo = b.outp("wo", [2, S, NH * HD])

    idf = b.sb([128, 128])
    o1f = b.sb([1, 128])
    exp_sb = b.sb([1, 6 * 128])
    off_sb = b.sb([1, 784])
    mk_sb = b.sb([128, 128])
    m16_sb = b.sb([128, 128])
    md_sb = b.sb([128, 128])
    td_sb = b.sb([1, NH * HD])
    tf_sb = b.sb([1, NH * HD])
    lnw = b.sb([1, NH * HD])

    DPOWi = [b.sb([HD, 128]) for _ in range(NH)]
    DPOW8 = [b.sb([HD, 128]) for _ in range(NH)]
    DPOW8K = [b.sb([HD, 128]) for _ in range(NH)]
    DPOWm8 = [b.sb([HD, 128]) for _ in range(NH)]
    DPOWm16 = [b.sb([HD, 128]) for _ in range(NH)]
    DTF = [b.sb([HD, 128]) for _ in range(NH)]
    OFF16 = [b.sb([HD, 784]) for _ in range(NH)]
    SPOWrev = [b.sb([128, HD]) for _ in range(NH)]
    w128c = [b.sb([HD, 1]) for _ in range(NH)]

    runit = b.sb([128, 32 * HD])
    kunit = b.sb([128, 32 * HD])
    vunit = b.sb([128, 32 * HD])
    vbf = b.sb([128, HD], BF16)
    kvbf = b.sb([128, HD], BF16)
    kT_sb = b.sb([HD, 128])
    Rtr = b.sb([HD, 128], BF16)
    Rm8 = b.sb([HD, 128], BF16)
    Rm16 = b.sb([HD, 128], BF16)
    Rtf = b.sb([HD, 128], BF16)
    Rint = b.sb([HD, 128], BF16)
    Ktr = b.sb([HD, 128], BF16)
    Kpl = b.sb([HD, 128], BF16)
    KOFF = b.sb([HD, 784], BF16)
    A_bf = b.sb([128, 128], BF16)
    tmp_bf = b.sb([128, 128], BF16)
    S_sb = b.sb([HD, HD])
    S_bf = b.sb([HD, HD], BF16)
    o_sb = b.sb([128, HD])

    At = b.ps([128, 128])
    o_ps = b.ps([128, HD])
    B_ps = b.ps([HD, HD])
    trR = b.ps([HD, 128])
    trK = b.ps([HD, 128])

    b.dma(idf[:], identf)
    b.dma(o1f[:], ones1f)
    b.dma(exp_sb[:], wexp)
    b.dma(off_sb[:], offpad)
    b.dma(mk_sb[:], mkeep)
    b.dma(m16_sb[:], m16off)
    b.dma(md_sb[:], mdiag)
    b.dma(td_sb[:], td8)
    b.dma(tf_sb[:], tf8)
    b.act(lnw[:], td_sb[:], AF.Exp)
    b.ts(lnw[:], lnw[:], -1.0, None, AL.mult)

    for h in range(NH):
        lr = lnw[0:1, h * HD:(h + 1) * HD]
        for (dst, row) in [(DPOWi[h], exp_sb[0:1, 0:128]), (DPOW8[h], exp_sb[0:1, 256:384]),
                           (DPOW8K[h], exp_sb[0:1, 384:512]), (DPOWm8[h], exp_sb[0:1, 512:640]),
                           (DPOWm16[h], exp_sb[0:1, 640:768])]:
            b.mm(At[0:HD, 0:128], lr, row, True, True)
            b.act(dst[:], At[0:HD, 0:128], AF.Exp)
        for j in range(7):
            b.mm(At[0:HD, 0:112], lr, off_sb[0:1, j * 112:(j + 1) * 112], True, True)
            b.act(OFF16[h][:, j * 112:(j + 1) * 112], At[0:HD, 0:112], AF.Exp)
        b.mm(At[0:HD, 0:128], tf_sb[0:1, h * HD:(h + 1) * HD], o1f[0:1, :], True, True)
        b.vcopy(DTF[h][:], At[0:HD, 0:128])
        b.mm(At[0:128, 0:HD], exp_sb[0:1, 128:256], lr, True, True)
        b.act(SPOWrev[h][:], At[0:128, 0:HD], AF.Exp)
        b.mm(At[0:HD, 0:1], lr, o1f[0:1, 0:1], True, True)
        b.act(w128c[h][:], At[0:HD, 0:1], AF.Exp, scale=128.0)

    for bb in range(2):
        for h in range(NH):
            b.memset(S_sb[:], 0.0)
            colsl = slice(h * HD, (h + 1) * HD)
            for src, dst in ((rr, runit), (kk, kunit), (vv, vunit)):
                for q in range(4):
                    b.dma(dst[:].rearrange("p (j d) -> p j d", j=NJ)[:, q * 8:(q + 1) * 8, :],
                          src[bb, q * 1024:(q + 1) * 1024, colsl]
                          .rearrange("(j p) d -> p j d", p=128))
            for j in range(NJ):
                rows = slice(j * 128, (j + 1) * 128)
                r_sb = runit[:, j * HD:(j + 1) * HD]
                k_sb = kunit[:, j * HD:(j + 1) * HD]
                v_sb = vunit[:, j * HD:(j + 1) * HD]
                b.scopy(vbf[:], v_sb)
                b.tt(kvbf[:], k_sb, SPOWrev[h][:], AL.mult)
                b.tr(trR[:], r_sb, idf[:])
                b.tr(trK[:], k_sb, idf[:])
                b.tt(Rtr[:], trR[:], DPOW8[h][:], AL.mult)
                b.tt(Rm8[:], trR[:], DPOWm8[h][:], AL.mult)
                b.tt(Rm16[:], trR[:], DPOWm16[h][:], AL.mult)
                b.tt(Rtf[:], trR[:], DTF[h][:], AL.mult)
                b.tt(Rint[:], trR[:], DPOWi[h][:], AL.mult)
                b.tt(Ktr[:], trK[:], DPOW8K[h][:], AL.mult)
                b.scopy(Kpl[:], trK[:])
                b.scopy(kT_sb[:], trK[:])
                b.tt(KOFF[:].rearrange("d (j t) -> d j t", j=7),
                     kT_sb[:, 0:112].unsqueeze(1).broadcast_to([HD, 7, 112]),
                     OFF16[h][:].rearrange("d (j t) -> d j t", j=7), AL.mult)
                b.mm(At[:], Ktr[:], Rtr[:], True, True)
                for J in range(1, 8):
                    b.mm(At[0:16 * J, 16 * J:16 * J + 16],
                         KOFF[:, (J - 1) * 112:(J - 1) * 112 + 16 * J],
                         Rm16[:, 16 * J:16 * J + 16], True, True)
                b.tt(A_bf[:], At[:], mk_sb[:], AL.mult)
                b.mm(At[:], Ktr[:], Rm8[:], True, True)
                b.tt(tmp_bf[:], At[:], m16_sb[:], AL.mult)
                b.tt(A_bf[:], A_bf[:], tmp_bf[:], AL.add)
                b.mm(At[:], Kpl[:], Rtf[:], True, True)
                b.tt(tmp_bf[:], At[:], md_sb[:], AL.mult)
                b.tt(A_bf[:], A_bf[:], tmp_bf[:], AL.add)
                b.mm(o_ps[:], A_bf[:], vbf[:], True, j == 0)
                if j > 0:
                    b.mm(o_ps[:], Rint[:], S_bf[:], False, True)
                b.scopy(o_sb[:], o_ps[:])
                b.dma(wo[bb, rows, colsl], o_sb[:])
                b.mm(B_ps[:], kvbf[:], vbf[:], True, True)
                b.stt(S_sb[:], S_sb[:], w128c[h][:, 0:1], B_ps[:], AL.mult, AL.add)
                b.scopy(S_bf[:], S_sb[:])
    return b.fin()


def build_rwkv_out():
    b = B()
    wkvi = b.inp("wkvi", [TOK, D])
    gi = b.inp("gi", [TOK, D])
    x4 = b.inp("x4", [TOK, D])
    gnw = b.inp("gnw", [1, D])
    gnb = b.inp("gnb", [1, D])
    woT = b.inp("woT", [D, D])
    identb = b.inp("identb", [128, 128], BF16)
    ones1f = b.inp("ones1f", [1, 128])
    x5 = b.outp("x5", [TOK, D])

    idb = b.sb([128, 128], BF16)
    o1f = b.sb([1, 128])
    wk_t = b.sb([128, D])
    g_t = b.sb([128, D])
    xc = b.sb([128, D])
    sq = b.sb([128, D])
    mu_g = b.sb([128, 64])
    var_g = b.sb([128, 64])
    gnwbc = b.sb([128, D])
    gnbbc = b.sb([128, D])
    tmp_row = b.sb([1, D])
    y_bf = b.sb([128, D], BF16)
    yT = b.sb([128, KD * TOK], BF16)
    wbuf = b.sb([128, 2048])
    w_bf = b.sb([128, 2048], BF16)
    xres = b.sb([128, 512])
    o_sb = b.sb([128, 512])

    tr_ps = b.ps([128, 128], BF16)
    bc_ps = b.ps([128, 512])
    mains = [b.ps([128, 512]) for _ in range(4)]

    b.dma(idb[:], identb)
    b.dma(o1f[:], ones1f)
    b.dma(tmp_row[:], gnw)
    bcast_row(b, bc_ps, tmp_row, o1f, gnwbc, D)
    b.dma(tmp_row[:], gnb)
    bcast_row(b, bc_ps, tmp_row, o1f, gnbbc, D)

    r3 = lambda ap: ap.rearrange("p (g i) -> p g i", g=64)
    for t in range(NT):
        b.dma(wk_t[:], wkvi[t * 128:(t + 1) * 128, :])
        b.dma(g_t[:], gi[t * 128:(t + 1) * 128, :])
        b.red(mu_g[:], r3(wk_t[:]), AL.add)
        b.ts(mu_g[:], mu_g[:], 1.0 / 64, None, AL.mult)
        b.tt(r3(xc[:]), r3(wk_t[:]), mu_g[:].unsqueeze(2).broadcast_to([128, 64, 64]), AL.subtract)
        b.act(sq[:], xc[:], AF.Square)
        b.red(var_g[:], r3(sq[:]), AL.add)
        b.ts(var_g[:], var_g[:], 1.0 / 64, 1e-5, AL.mult, AL.add)
        b.act(var_g[:], var_g[:], AF.Ln)
        b.act(var_g[:], var_g[:], AF.Exp, scale=-0.5)
        b.tt(r3(xc[:]), r3(xc[:]), var_g[:].unsqueeze(2).broadcast_to([128, 64, 64]), AL.mult)
        b.tt(xc[:], xc[:], gnwbc[:], AL.mult)
        b.tt(xc[:], xc[:], gnbbc[:], AL.add)
        b.tt(y_bf[:], xc[:], g_t[:], AL.mult)
        for k in range(KD):
            b.tr(tr_ps[:], y_bf[:, k * 128:(k + 1) * 128], idb[:])
            b.vcopy(yT[:, k * TOK + t * 128: k * TOK + (t + 1) * 128], tr_ps[:])

    for tg in range(2):
        tl = list(range(tg * 4, tg * 4 + 4))
        for ob in range(8):
            for kg in range(KD // 4):
                b.dma(wbuf[:].rearrange("p (k o) -> p k o", k=4),
                      woT[kg * 512:(kg + 1) * 512, ob * 512:(ob + 1) * 512]
                      .rearrange("(k p) o -> p k o", p=128))
                b.scopy(w_bf[:], wbuf[:])
                for kk in range(4):
                    k = kg * 4 + kk
                    for i, t in enumerate(tl):
                        b.mm(mains[i][:], yT[:, k * TOK + t * 128: k * TOK + (t + 1) * 128],
                             w_bf[:, kk * 512:(kk + 1) * 512], k == 0, k == KD - 1)
            for i, t in enumerate(tl):
                b.dma(xres[:], x4[t * 128:(t + 1) * 128, ob * 512:(ob + 1) * 512])
                b.tt(o_sb[:], mains[i][:], xres[:], AL.add)
                b.dma(x5[t * 128:(t + 1) * 128, ob * 512:(ob + 1) * 512], o_sb[:])
    return b.fin()


_cache = {}


def _get(name, fn):
    if name not in _cache:
        _cache[name] = fn()
    return _cache[name]


def _run(nc, in_maps):
    res = run_bass_kernel_spmd(nc, in_maps, core_ids=list(range(NCORE)))
    return res.results


def _consts():
    c = {}
    c['identb'] = np.eye(128).astype(BF)
    c['identf'] = np.eye(128, dtype=np.float32)
    c['ones1f'] = np.ones((1, 128), dtype=np.float32)
    c['ones1b'] = np.ones((1, 128)).astype(BF)
    return c


def _shards(x):
    return [np.ascontiguousarray(x[c * TOK:(c + 1) * TOK]) for c in range(NCORE)]


def _gather(res, key):
    return np.concatenate([np.asarray(r[key], dtype=np.float32) for r in res], axis=0)


def launch_ffn(x, wT, gwT, acatT, bcatT, eb, bb, cfg, consts):
    nc = _get(f"ffn{cfg}", lambda: build_ffn(*cfg))
    com = dict(wT=wT, gwT=gwT, acatT=acatT, bcatT=bcatT, eb=eb, bb=bb,
               identb=consts['identb'], ones1f=consts['ones1f'], ones1b=consts['ones1b'])
    maps = [dict(xin=s, **com) for s in _shards(x)]
    return _gather(_run(nc, maps), "xout")


def kernel(_debug_cb=None, **inp):
    _cb = _debug_cb if _debug_cb is not None else (lambda *a: None)
    consts = _consts()
    f32 = np.float32
    x = np.ascontiguousarray(np.asarray(inp["L0_attn_x"], f32).reshape(2 * 4096, D))

    l0_wT = np.ascontiguousarray(np.asarray(inp["L0_ffn_W_up"], f32).T)
    l0_gwT = np.ascontiguousarray(np.asarray(inp["L0_ffn_gate_w"], f32).T)
    l0_eb = np.ascontiguousarray(np.asarray(inp["L0_ffn_expert_bias"], f32).reshape(1, 4))
    l0_acatT = np.ascontiguousarray(np.asarray(inp["L0_ffn_lora_A"], f32).reshape(32, D).T)
    l0_bcatT = np.ascontiguousarray(
        np.asarray(inp["L0_ffn_lora_B"], f32).transpose(0, 2, 1).reshape(32, D))
    l0_bb = np.zeros((1, D), f32)
    cfg0 = (4, 8, 1, 1.0, True)

    l1_wT = np.ascontiguousarray(np.asarray(inp["L1_ffn_base_w"], f32).T)
    l1_gwT = np.ascontiguousarray(np.asarray(inp["L1_ffn_gate_w"], f32).T)
    l1_eb = np.zeros((1, 8), f32)
    l1_acatT = np.ascontiguousarray(np.asarray(inp["L1_ffn_lora_A"], f32).reshape(128, D).T)
    l1_bcatT = np.ascontiguousarray(
        np.asarray(inp["L1_ffn_lora_B"], f32).transpose(0, 2, 1).reshape(128, D))
    l1_bb = np.ascontiguousarray(np.asarray(inp["L1_ffn_base_b"], f32).reshape(1, D))
    cfg1 = (8, 16, 2, 2.0, False)

    x1 = launch_ffn(x, l0_wT, l0_gwT, l0_acatT, l0_bcatT, l0_eb, l0_bb, cfg0, consts)

    _cb("x1", x1)
    nc2 = _get("a0in", build_attn0_in)
    winT = np.ascontiguousarray(np.asarray(inp["L0_attn_Win"], f32).T)
    binr = np.ascontiguousarray(np.asarray(inp["L0_attn_bin"], f32).reshape(1, 512))
    ln1w = np.asarray(inp["L0_attn_ln1_w"], f32)
    ln1b = np.asarray(inp["L0_attn_ln1_b"], f32)
    com = dict(winT=winT, binr=binr,
               ln1w=np.ascontiguousarray(ln1w[0].reshape(1, 512)),
               ln1b=np.ascontiguousarray(ln1b[0].reshape(1, 512)),
               identb=consts['identb'], ones1f=consts['ones1f'], ones1b=consts['ones1b'])
    res = _run(nc2, [dict(xin=s, **com) for s in _shards(x1)])
    h = _gather(res, "h0")
    y = _gather(res, "y0")
    _cb("h0", h)
    _cb("y0", y)

    nc_scan = _get("scan", build_scan)
    nc_mlp = _get("mlp", build_mlp)
    phr = np.asarray(inp["L0_attn_phz_re"], f32)
    phi = np.asarray(inp["L0_attn_phz_im"], f32)
    lcr = np.asarray(inp["L0_attn_lc_re"], f32)
    lci = np.asarray(inp["L0_attn_lc_im"], f32)
    scl = np.asarray(inp["L0_attn_sc_lin"], f32)
    ln2w = np.asarray(inp["L0_attn_ln2_w"], f32)
    ln2b = np.asarray(inp["L0_attn_ln2_b"], f32)
    W1 = np.asarray(inp["L0_attn_W1"], f32)
    b1 = np.asarray(inp["L0_attn_b1"], f32)
    W2 = np.asarray(inp["L0_attn_W2"], f32)
    b2 = np.asarray(inp["L0_attn_b2"], f32)
    for t in range(4):
        maps = []
        for c in range(NCORE):
            bb_, cg = c // 4, c % 4
            sl = slice(cg * 128, (cg + 1) * 128)
            maps.append(dict(
                ycs=np.ascontiguousarray(y[bb_ * 4096:(bb_ + 1) * 4096, sl]),
                phr=np.ascontiguousarray(phr[t, sl].reshape(128, 1)),
                phi=np.ascontiguousarray(phi[t, sl].reshape(128, 1)),
                lcr=np.ascontiguousarray(lcr[t, sl].reshape(128, 1)),
                lci=np.ascontiguousarray(lci[t, sl].reshape(128, 1)),
                identf=consts['identf']))
        res = _run(nc_scan, maps)
        conv = np.zeros((8192, 512), f32)
        for c in range(NCORE):
            bb_, cg = c // 4, c % 4
            conv[bb_ * 4096:(bb_ + 1) * 4096, cg * 128:(cg + 1) * 128] = \
                np.asarray(res[c]["convo"], f32)
        tn = min(t + 1, 3)
        com = dict(sclr=np.ascontiguousarray(scl[t].reshape(1, 512)),
                   ln2w=np.ascontiguousarray(ln2w[t].reshape(1, 512)),
                   ln2b=np.ascontiguousarray(ln2b[t].reshape(1, 512)),
                   w1T=np.ascontiguousarray(W1[t].T), b1r=np.ascontiguousarray(b1[t].reshape(1, 2048)),
                   w2T=np.ascontiguousarray(W2[t].T), b2r=np.ascontiguousarray(b2[t].reshape(1, 512)),
                   lnnw=np.ascontiguousarray(ln1w[tn].reshape(1, 512)),
                   lnnb=np.ascontiguousarray(ln1b[tn].reshape(1, 512)),
                   identb=consts['identb'], ones1f=consts['ones1f'], ones1b=consts['ones1b'])
        hs = _shards(h)
        cs = _shards(conv)
        _cb(f"conv{t}", conv)
        res = _run(nc_mlp, [dict(hin=hs[c], cin=cs[c], **com) for c in range(NCORE)])
        h = _gather(res, "hout")
        y = _gather(res, "ynext")
        _cb(f"h{t + 1}", h)
        _cb(f"y{t + 1}", y)

    nc_f = _get("a0fin", build_attn0_final)
    com = dict(lnlw=np.ascontiguousarray(np.asarray(inp["L0_attn_lnl_w"], f32).reshape(1, 512)),
               lnlb=np.ascontiguousarray(np.asarray(inp["L0_attn_lnl_b"], f32).reshape(1, 512)),
               woutT=np.ascontiguousarray(np.asarray(inp["L0_attn_Wout"], f32).T),
               boutr=np.ascontiguousarray(np.asarray(inp["L0_attn_bout"], f32).reshape(1, D)),
               identb=consts['identb'], ones1f=consts['ones1f'], ones1b=consts['ones1b'])
    hs = _shards(h)
    x1s = _shards(x1)
    res = _run(nc_f, [dict(hin=hs[c], x1=x1s[c], **com) for c in range(NCORE)])
    x2 = _gather(res, "x2")
    _cb("x2", x2)

    x3 = launch_ffn(x2, l0_wT, l0_gwT, l0_acatT, l0_bcatT, l0_eb, l0_bb, cfg0, consts)
    _cb("x3", x3)
    x4 = launch_ffn(x3, l1_wT, l1_gwT, l1_acatT, l1_bcatT, l1_eb, l1_bb, cfg1, consts)
    _cb("x4", x4)

    nc_p = _get("rwkvp", build_rwkv_proj)
    w_tm = {}
    for nm, key in [("tmk", "L1_attn_time_mix_key"), ("tmv", "L1_attn_time_mix_value"),
                    ("tmr", "L1_attn_time_mix_receptance"), ("tmg", "L1_attn_time_mix_gate")]:
        w_tm[nm] = np.ascontiguousarray(np.asarray(inp[key], f32).reshape(KD, 128).T)
    com = dict(wkT=np.ascontiguousarray(np.asarray(inp["L1_attn_w_key"], f32).T),
               wvT=np.ascontiguousarray(np.asarray(inp["L1_attn_w_value"], f32).T),
               wrT=np.ascontiguousarray(np.asarray(inp["L1_attn_w_receptance"], f32).T),
               wgT=np.ascontiguousarray(np.asarray(inp["L1_attn_w_gate"], f32).T),
               identf=consts['identf'], **w_tm)
    x4s = _shards(x4)
    maps = []
    for c in range(NCORE):
        if c % 4 == 0:
            halo = np.zeros((1, D), f32)
        else:
            halo = np.ascontiguousarray(x4[c * TOK - 1:c * TOK, :])
        maps.append(dict(xin=x4s[c], halo=halo, **com))
    res = _run(nc_p, maps)
    kf = _gather(res, "ko")
    vf = _gather(res, "vo")
    rf = _gather(res, "ro")
    gf = _gather(res, "go")
    _cb("kf", kf)
    _cb("vf", vf)
    _cb("rf", rf)
    _cb("gf", gf)

    nc_w = _get("wkv", build_wkv)
    td = np.asarray(inp["L1_attn_time_decay"], f32)
    tf = np.asarray(inp["L1_attn_time_faaaa"], f32)
    t_ar = np.arange(128, dtype=f32)
    wexp = np.stack([t_ar, 127 - t_ar, (t_ar % 8) - 8, 7 - (t_ar % 8),
                     t_ar % 8, t_ar % 16]).astype(f32).reshape(1, 768)
    offpad = np.full((1, 784), 300.0, f32)
    for J in range(1, 8):
        tt_ = np.arange(16 * J, dtype=f32)
        offpad[0, (J - 1) * 112:(J - 1) * 112 + 16 * J] = 16 * J - 1 - tt_
    ti = np.arange(128)
    tgrid, igrid = np.meshgrid(ti, ti, indexing='ij')
    mkeep = ((tgrid < igrid) & ~((tgrid // 16 == igrid // 16) & (tgrid % 16 < 8)
                                 & (igrid % 16 >= 8))).astype(f32)
    m16off = ((tgrid // 16 == igrid // 16) & (tgrid % 16 < 8) & (igrid % 16 >= 8)).astype(f32)
    mdiag = np.eye(128, dtype=f32)
    k3 = kf.reshape(2, 4096, D)
    v3 = vf.reshape(2, 4096, D)
    r3_ = rf.reshape(2, 4096, D)
    maps = []
    for c in range(NCORE):
        sl = slice(c * 512, (c + 1) * 512)
        hsl = slice(c * 8, (c + 1) * 8)
        maps.append(dict(kk=np.ascontiguousarray(k3[:, :, sl]),
                         vv=np.ascontiguousarray(v3[:, :, sl]),
                         rr=np.ascontiguousarray(r3_[:, :, sl]),
                         td8=np.ascontiguousarray(td[hsl].reshape(1, 512)),
                         tf8=np.ascontiguousarray(tf[hsl].reshape(1, 512)),
                         wexp=wexp, offpad=offpad, mkeep=mkeep, m16off=m16off, mdiag=mdiag,
                         identf=consts['identf'], ones1f=consts['ones1f']))
    res = _run(nc_w, maps)
    wkv = np.zeros((2, 4096, D), f32)
    for c in range(NCORE):
        wkv[:, :, c * 512:(c + 1) * 512] = np.asarray(res[c]["wo"], f32)
    wkv = wkv.reshape(8192, D)
    _cb("wkv", wkv)

    nc_o = _get("rwkvo", build_rwkv_out)
    com = dict(gnw=np.ascontiguousarray(np.asarray(inp["L1_attn_gn_weight"], f32).reshape(1, D)),
               gnb=np.ascontiguousarray(np.asarray(inp["L1_attn_gn_bias"], f32).reshape(1, D)),
               woT=np.ascontiguousarray(np.asarray(inp["L1_attn_w_output"], f32).T),
               identb=consts['identb'], ones1f=consts['ones1f'])
    ws = _shards(wkv)
    gs = _shards(gf)
    res = _run(nc_o, [dict(wkvi=ws[c], gi=gs[c], x4=x4s[c], **com) for c in range(NCORE)])
    x5 = _gather(res, "x5")
    _cb("x5", x5)

    x6 = launch_ffn(x5, l1_wT, l1_gwT, l1_acatT, l1_bcatT, l1_eb, l1_bb, cfg1, consts)
    _cb("x6", x6)
    return x6.reshape(2, 4096, D)
